# version 14
# baseline (speedup 1.0000x reference)
import contextlib
import numpy as np
import ml_dtypes

import concourse.bass as bass
import concourse.mybir as mybir
from concourse.bass_utils import run_bass_kernel_spmd

F32 = mybir.dt.float32
BF16 = mybir.dt.bfloat16
I32 = mybir.dt.int32
AF = mybir.ActivationFunctionType
ALU = mybir.AluOpType
AX = mybir.AxisListType
BF = ml_dtypes.bfloat16

NCORES = 8
BPC = 2
S = 2048
D = 1024
NT = S // 128
NF = D // 128

ENG_ATTR = {"T": "tensor", "V": "vector", "S": "scalar", "G": "gpsimd", "Y": "sync"}


class Prog:
    def __init__(self, nc, stack):
        self.nc = nc
        self.q = {e: [] for e in ENG_ATTR}
        self.sem = {}
        for name in ["T", "V", "S", "G", "Y", "YD", "GD", "SD"]:
            self.sem[name] = stack.enter_context(nc.semaphore("sem" + name))
        self.cnt = {k: 0 for k in self.sem}
        self.waited = {e: {} for e in ENG_ATTR}

    @staticmethod
    def _flat(deps):
        for d in deps:
            if d is None:
                continue
            if isinstance(d, tuple) and len(d) == 2 and isinstance(d[0], str):
                yield d
            else:
                yield from Prog._flat(d)

    def _wait(self, eng, tok):
        if tok is None:
            return
        sem_name, val = tok
        if self.waited[eng].get(sem_name, 0) >= val:
            return
        self.waited[eng][sem_name] = val
        sem = self.sem[sem_name]
        self.q[eng].append(lambda e, sem=sem, val=val: e.wait_ge(sem, val))

    def step(self, eng, fn, deps=()):
        if eng in ("V", "S", "G") and self.cnt[eng]:
            self._wait(eng, (eng, self.cnt[eng]))
        for d in self._flat(deps):
            self._wait(eng, d)
        self.cnt[eng] += 1
        sem = self.sem[eng]

        def thunk(e, fn=fn, sem=sem, eng=eng):
            inst = fn(e)
            assert inst is not None, "step fn must return its last instruction"
            inst.then_inc(sem, 1)
        self.q[eng].append(thunk)
        return (eng, self.cnt[eng])

    def dma(self, eng, out, in_, deps=(), slow=False):
        sem_name = {"Y": "YD", "G": "GD", "S": "SD"}[eng]
        if self.cnt[sem_name]:
            self._wait(eng, (sem_name, self.cnt[sem_name]))
        for d in self._flat(deps):
            self._wait(eng, d)
        self.cnt[sem_name] += 16
        sem = self.sem[sem_name]
        self.q[eng].append(
            lambda e, out=out, in_=in_, sem=sem, slow=slow: e.dma_start(
                out=out, in_=in_, allow_slow_non_contiguous=slow).then_inc(sem, 16))
        return (sem_name, self.cnt[sem_name])

    def waits(self, eng, deps):
        for d in self._flat(deps):
            self._wait(eng, d)

    def dma_fn(self, eng, fn, deps=(), chain=True):
        sem_name = {"Y": "YD", "G": "GD", "S": "SD"}[eng]
        if chain and self.cnt[sem_name]:
            self._wait(eng, (sem_name, self.cnt[sem_name]))
        for d in self._flat(deps):
            self._wait(eng, d)
        self.cnt[sem_name] += 16
        sem = self.sem[sem_name]
        self.q[eng].append(
            lambda e, fn=fn, sem=sem: fn(e).then_inc(sem, 16))
        return (sem_name, self.cnt[sem_name])

    def barrier(self, engs="TVSGY"):
        toks = [(e, self.cnt[e]) for e in engs if self.cnt[e]] +                [(q, self.cnt[q]) for q in ("YD", "GD", "SD") if self.cnt[q]]
        for e in engs:
            for t in toks:
                self._wait(e, t)

    def finalize(self, block):
        for eng, attr in ENG_ATTR.items():
            thunks = self.q[eng]

            def section(e, thunks=thunks):
                for t in thunks:
                    t(e)
            getattr(block, attr)(section)


def ts(i, n=128):
    return slice(i * n, (i + 1) * n)


def host_constants():
    c = {}
    c["identity_f32"] = np.ascontiguousarray(np.eye(128, dtype=np.float32))
    c["ones_1x128_f32"] = np.ones((1, 128), np.float32)
    c["blockones2"] = np.zeros((128, 2), np.float32)
    c["blockones2"][:64, 0] = 1.0
    c["blockones2"][64:, 1] = 1.0
    c["sel2"] = np.zeros((2, 128), np.float32)
    c["sel2"][0, :64] = 1.0
    c["sel2"][1, 64:] = 1.0
    j = np.arange(S, dtype=np.float32)
    j_hi = np.floor(j / 128) * 128
    j_lo = j - j_hi
    j4 = np.stack([j_hi, j_lo, j_hi, j_lo])
    j4pack = np.zeros((128, S), np.float32)
    for a in range(4):
        j4pack[32 * a:32 * a + 4] = j4
    c["j4pack"] = j4pack
    c["i_local"] = np.arange(128, dtype=np.float32).reshape(1, 128)
    c["qtramp"] = ((np.arange(256, dtype=np.float32) % 16) * 128.0).reshape(1, 256)
    m = np.zeros((128, 128), np.float32)
    ii, jj = np.mgrid[0:128, 0:128]
    m[jj > ii] = -1e30
    c["mmask"] = m
    c["pcol"] = np.arange(128, dtype=np.float32).reshape(128, 1)
    return c


class Core:
    def __init__(self, phases=("A", "B", "C", "D"), dumps=(), batches=BPC):
        self.phases = phases
        self.dumps = dumps
        self.batches = batches
        self.nc = bass.Bass("TRN2", target_bir_lowering=False)
        self.stack = contextlib.ExitStack()
        self.p = Prog(self.nc, self.stack)
        self.dt = {}
        self.sb = {}
        self.free = {}
        self.store_toks = []
        self._declare_dram()
        self._alloc_persistent()
        self._emit()

    def _din(self, name, shape, dtype=F32):
        self.dt[name] = self.nc.dram_tensor(name, list(shape), dtype, kind="ExternalInput").ap()

    def _dout(self, name, shape, dtype=F32):
        self.dt[name] = self.nc.dram_tensor(name, list(shape), dtype, kind="ExternalOutput").ap()

    def _declare_dram(self):
        din = self._din
        din("x", (self.batches, 128, NT * D))
        for n in ["Wq", "Wk", "Wv", "Wo"]:
            din(n, (128, 8, D))
        din("W1f", (32, 128, 8, 128))
        din("W2", (128, 32, D))
        for n in ["bq", "bk", "bv", "bo", "b2", "bv1", "bproj"]:
            din(n + "_row", (1, D))
        din("b1_pp", (128, 32))
        for n in ["g1", "be1", "g2", "be2"]:
            din(n + "_bc", (128, D))
        din("choice_wT", (128, 8, 8))
        din("moe_w1f", (8, 128, 16, 8, 128))
        din("moe_w2", (8, 128, 16, D))
        din("Wkqv_f", (16, 128, 8, 128))
        din("Wv1", (128, 8, D))
        din("Wproj", (128, 8, D))
        din("bk1_pp", (128, 8))
        din("bq1_pp", (128, 8))
        din("slopes_row", (1, 16))
        din("f_w1", (4096, D))
        din("f_w2", (D, 4096))
        din("f_convw_pp", (128, 3, 32))
        din("f_convb_pp", (128, 32))
        for k, v in host_constants().items():
            din(k, v.shape)
        self.dt["q1t_scr"] = self.nc.dram_tensor("q1t_scr", [128, 8, 4096], BF16).ap()
        self.dt["q2t_scr"] = self.nc.dram_tensor("q2t_scr", [128, 32, 1024], BF16).ap()
        self.dt["h2_scr"] = self.nc.dram_tensor("h2_scr", [32, 128, 2048], F32).ap()
        self.dt["deq_scr"] = self.nc.dram_tensor("deq_scr", [16, 128], F32).ap()
        self.dt["thr_scr"] = self.nc.dram_tensor("thr_scr", [1, 8], F32).ap()

        self.dt["xs_scr"] = self.nc.dram_tensor("xs_scr", [8, 512, 1024], BF16).ap()
        self.dt["y_scr"] = self.nc.dram_tensor("y_scr", [2048, 1024], F32).ap()
        self.dt["gid_scr"] = self.nc.dram_tensor("gid_scr", [8, 512, 2], F32).ap()
        self.dt["moe_w1b"] = self.nc.dram_tensor(
            "moe_w1b", [8, 128, 16, 8, 128], BF16).ap()
        self.dt["moe_w2b"] = self.nc.dram_tensor(
            "moe_w2b", [8, 128, 16, 1024], BF16).ap()
        self._dout("out", (self.batches, 128, NT * D))
        for d in self.dumps:
            self._dout("dump_" + d, (self.batches, 128, NT * D))

    def _uniq(self, name):
        self._uid = getattr(self, "_uid", 0) + 1
        return f"{name}_u{self._uid}"

    def _sbuf(self, name, shape, dtype, stack=None):
        t = (stack or self.stack).enter_context(
            self.nc.sbuf_tensor(self._uniq(name), list(shape), dtype))
        self.sb[name] = t
        return t

    def _alloc_persistent(self):
        sbuf = self._sbuf
        self.XR = sbuf("XR", (128, NT, D), F32)
        self.X0B = sbuf("X0B", (128, NT, D), BF16)
        self.IDT = sbuf("IDT", (128, 128), F32)
        self.ONES1 = sbuf("ONES1", (1, 128), BF16)
        self.ONES1F = sbuf("ONES1F", (1, 128), F32)
        self.ONESR = sbuf("ONESR", (1, 512), BF16)
        self.BO2 = sbuf("BO2", (128, 2), F32)
        self.SEL2 = sbuf("SEL2", (2, 128), BF16)
        self.ROWS = sbuf("ROWS", (1, 5 * D), BF16)
        self.BPP = sbuf("BPP", (128, 128), F32)
        self.SMALL = sbuf("SMALL", (128, 64), F32)
        self.TMP = sbuf("TMP", (128, 2, 1024), F32)
        self.SCR = sbuf("SCR", (128, 1024), F32)
        self.IDTB = sbuf("IDTB", (128, 128), BF16)
        self.SCONST = sbuf("SCONST", (128, 4, 128), BF16)
        self.SLP = sbuf("SLP", (1, 128), F32)
        self.SLB = sbuf("SLB", (1, 128), BF16)
        self.SWB = sbuf("SWB", (128, 4), F32)
        self.CWPP = sbuf("CWPP", (128, 3, 32), F32)
        self.ONE128 = sbuf("ONE128", (128, 1), F32)

    def _phase_reset(self):
        self.free = {k: v for k, v in self.free.items() if k in ("xr_store",)}

    def dump_xr(self, name, deps):
        if "dump_" + name not in self.dt:
            return
        tok = self.p.dma("G", self.dt["dump_" + name][self._cur_batch],
                         self.XR[:].rearrange("p t d -> p (t d)"), deps=deps)
        self.p._wait("G", tok)
        self.store_toks.append(tok)

    def load_consts(self):
        p, dt = self.p, self.dt
        ld = []
        ld.append(p.dma("Y", self.IDT[:], dt["identity_f32"]))
        ld.append(p.dma("Y", self.ONES1F[:], dt["ones_1x128_f32"]))
        ld.append(p.dma("G", self.ONES1[:], dt["ones_1x128_f32"]))
        ld.append(p.dma("Y", self.BO2[:], dt["blockones2"]))
        ld.append(p.dma("G", self.SEL2[:], dt["sel2"]))
        self.ROW_SLOT = {}
        ld.append(p.dma("Y", self.BPP[:, 0:32], dt["b1_pp"]))
        ld.append(p.dma("Y", self.BPP[:, 32:40], dt["bk1_pp"]))
        ld.append(p.dma("Y", self.BPP[:, 40:48], dt["bq1_pp"]))
        ld.append(p.dma("Y", self.BPP[:, 48:80], dt["f_convb_pp"]))
        ld.append(p.dma("G", self.IDTB[:], dt["identity_f32"]))
        ld.append(p.dma("Y", self.SLP[:, 0:16], dt["slopes_row"]))
        ld.append(p.dma("Y", self.CWPP[:], dt["f_convw_pp"]))
        t = p.step("V", lambda e: e.memset(self.ONESR[:], 1.0))
        t2 = p.step("V", lambda e: e.memset(self.ONE128[:], 1.0))
        self.tok_const = tuple(ld) + (t, t2)
        self._gen_attn_consts()
        self._emit_w0()

    def _emit_moe_cast(self, CH):
        p, dt, nc = self.p, self.dt, self.nc
        if True:
            ci = 0
            for e_i in range(8):
                for c in range(4):
                    for src, dst in (
                        (dt["moe_w1f"][e_i][:, 4 * c:4 * c + 4],
                         dt["moe_w1b"][e_i][:, 4 * c:4 * c + 4]),
                        (dt["moe_w2"][e_i][:, 4 * c:4 * c + 4, :],
                         dt["moe_w2b"][e_i][:, 4 * c:4 * c + 4, :]),
                    ):
                        b = CH[ci % 2]
                        v = b[:].rearrange("p (a w) -> p a w", a=4)
                        ld = p.dma("G", v, src,
                                   deps=[self.free.get(("mcch", ci % 2))])
                        stk = p.dma("Y", dst, v, deps=[ld])
                        self.free[("mcch", ci % 2)] = stk
                        ci += 1
            self._mc_tok = stk

    def _gen_attn_consts(self):
        p, nc = self.p, self.nc
        with contextlib.ExitStack() as st:
            ps = st.enter_context(nc.psum_tensor("CG_ps", [128, 256], F32))
            cd = [self.tok_const]
            t = p.step("V", lambda e: e.tensor_copy(self.SLB[:, 0:16], self.SLP[:, 0:16]),
                       deps=cd)
            t = p.step("V", lambda e: e.tensor_copy(self.SLP[:, 16:32], self.SLB[:, 0:16]),
                       deps=[t])
            t = p.step("V", lambda e: e.tensor_tensor(
                self.SLP[:, 32:48], self.SLP[:, 0:16], self.SLP[:, 16:32], ALU.subtract),
                deps=[t])
            t = p.step("V", lambda e: e.tensor_copy(self.SLB[:, 16:32], self.SLP[:, 32:48]),
                       deps=[t])
            sh4 = self.SLB[:, 64:128].rearrange("o (h r) -> o h r", r=4)
            for j, src in [(0, 16), (1, 16), (2, 16 + 16), (3, 16 + 16)]:
                srcsl = self.SLB[:, 0:16] if j < 2 else self.SLB[:, 16:32]
                t = p.step("V", lambda e, j=j, srcsl=srcsl: e.tensor_copy(
                    sh4[:, :, j:j + 1], srcsl.unsqueeze(2)), deps=[t])
            gt = t
            for h in range(16):
                b = h % 4
                mt = p.step("T", lambda e, h=h, b=b: e.matmul(
                    ps[32 * b:32 * b + 4, 0:128], self.SLB[:, 64 + 4 * h:64 + 4 * h + 4],
                    self.ONES1[:, 0:128], start=True, stop=True,
                    tile_position=(0, 32 * b), skip_group_check=True), deps=[gt])
                gt = p.step("V", lambda e, h=h, b=b: e.tensor_copy(
                    self.SCONST[32 * b:32 * b + 4, h // 4, :], ps[32 * b:32 * b + 4, 0:128]),
                    deps=[mt])
            self.tok_const = self.tok_const + (gt,)
        p.barrier()

    def load_rows(self, names):
        self.ROW_SLOT = {}
        toks = []
        for i, n in enumerate(names):
            self.ROW_SLOT[n] = i
            toks.append(self.p.dma("G", self.ROWS[:, ts(i, D)], self.dt[n + "_row"]))
        self.rows_tok = tuple(toks)
        return self.rows_tok

    def rank1(self, e, psum_ap, slot, n=D, stop=True):
        row = self.ROWS[:, ts(slot, D)]
        last = None
        for h in range(0, n, 512):
            w = min(512, n - h)
            last = e.matmul(psum_ap[:, h:h + w], self.ONES1[:, 0:128],
                            row[:, h:h + w], start=False, stop=stop)
        return last

    def rank1_fm(self, e, psum_ap, slot, ftile, ntok, stop=True):
        row = self.ROWS[:, ts(slot, D)][:, ts(ftile)]
        return e.matmul(psum_ap, row, self.ONESR[:, 0:ntok], start=False, stop=stop)

    def transpose_into(self, src_ap_fn, dst, dstcol, ps, deps, bank="ps_t", src_tb_base=0):
        p = self.p
        last_ev = None
        for fc in range(NF):
            tk = p.step("T", lambda e, fc=fc: [
                e.transpose(ps[:, ts(tt)], src_ap_fn(src_tb_base + tt, fc), self.IDT[:])
                for tt in range(4)][-1],
                deps=list(deps) + [self.free.get(bank)])
            ev = p.step("V", lambda e, fc=fc: e.tensor_copy(
                dst[:, fc, dstcol:dstcol + 512], ps[:]), deps=[tk])
            self.free[bank] = ev
            last_ev = ev
        return last_ev

    def layernorm(self, src, dep_toks, eps, gamma=None, beta=None, dst=None,
                  add_src=None, extra_deps=()):
        p = self.p
        if dst is None:
            dst = src
        SM = self.SMALL
        stat_toks = []
        for tb in range(NT):
            d = [dep_toks[tb]] if isinstance(dep_toks, (list, tuple)) else [dep_toks]
            t1 = p.step("V", lambda e, tb=tb: e.scalar_tensor_tensor(
                self.SCR[:], src[:, tb, :], 1.0, src[:, tb, :], ALU.bypass, ALU.mult,
                accum_out=SM[:, 16 + tb:17 + tb]), deps=d)
            t2 = p.step("V", lambda e, tb=tb: e.reduce_sum(
                SM[:, tb:tb + 1], src[:, tb, :], AX.X), deps=d)
            stat_toks += [t1, t2]

        tf = p.step("V", lambda e: e.tensor_scalar(
            SM[:, 32:48], SM[:, 0:16], -1.0 / D, None, ALU.mult), deps=stat_toks)
        tf = p.step("V", lambda e: e.tensor_tensor(
            self.SCR[:, 0:16], SM[:, 32:48], SM[:, 32:48], ALU.mult), deps=[tf])
        tf = p.step("V", lambda e: e.scalar_tensor_tensor(
            self.SCR[:, 16:32], SM[:, 16:32], 1.0 / D, self.SCR[:, 0:16],
            ALU.mult, ALU.subtract), deps=[tf])
        tf = p.step("V", lambda e: e.tensor_scalar(
            self.SCR[:, 16:32], self.SCR[:, 16:32], eps, None, ALU.add), deps=[tf])
        tf = p.step("V", lambda e: e.reciprocal(
            self.SCR[:, 32:48], self.SCR[:, 16:32]), deps=[tf])
        tr = p.step("S", lambda e: e.activation(SM[:, 48:64], self.SCR[:, 32:48], AF.Sqrt),
                    deps=[tf])
        out_toks = []
        for tb in range(NT):
            simple = gamma is None and add_src is None
            t = p.step("V", lambda e, tb=tb, simple=simple: e.tensor_scalar(
                dst[:, tb, :] if simple else self.TMP[:, 0, :],
                src[:, tb, :], SM[:, 32 + tb:33 + tb], SM[:, 48 + tb:49 + tb],
                ALU.add, ALU.mult), deps=[tr] + list(extra_deps))
            if gamma is not None:
                t = p.step("V", lambda e: e.tensor_tensor(
                    self.TMP[:, 1, :], self.TMP[:, 0, :], gamma, ALU.mult), deps=[t])
                if add_src is None:
                    t = p.step("V", lambda e, tb=tb: e.tensor_tensor(
                        dst[:, tb, :], self.TMP[:, 1, :], beta, ALU.add), deps=[t])
                else:
                    t = p.step("V", lambda e, tb=tb: e.tensor_tensor(
                        self.TMP[:, 0, :], self.TMP[:, 1, :], beta, ALU.add), deps=[t])
                    t = p.step("V", lambda e, tb=tb: e.tensor_tensor(
                        dst[:, tb, :], self.TMP[:, 0, :], add_src[:, tb, :], ALU.add),
                        deps=[t])
            elif add_src is not None:
                t = p.step("V", lambda e, tb=tb: e.tensor_tensor(
                    dst[:, tb, :], self.TMP[:, 0, :], add_src[:, tb, :], ALU.add), deps=[t])
            out_toks.append(t)
        return out_toks

    def emit_phaseA(self, x_toks):
        self._phase_reset()
        p, dt, nc = self.p, self.dt, self.nc
        with contextlib.ExitStack() as st:
            def psum(name, w):
                return st.enter_context(nc.psum_tensor(self._uniq(name), [128, w], F32))
            sbuf = lambda n, sh, dty: self._sbuf(n, sh, dty, st)

            ps_t = psum("A_pst", 512)
            ps_q = [psum("A_psq0", 512), psum("A_psq1", 512)]
            ps_o = [psum("A_pso0", 1024), psum("A_pso1", 1024)]
            ps_kv = psum("A_pskv", 512)
            ps_ss = ps_t

            TMPB = sbuf("A_TMPB", (128, 2, 1024), F32)
            TMP, SCR = self.TMP, self.SCR
            SROW = sbuf("SROW", (2, 512), F32)
            SROWB = sbuf("SROWB", (2, 512), BF16)
            x_dep = list(x_toks) if isinstance(x_toks, (list, tuple)) else [x_toks]

            x0b_toks = [p.step("V", lambda e, tb=tb: e.tensor_copy(
                self.X0B[:, tb, :], self.XR[:, tb, :]), deps=x_dep) for tb in range(NT)]
            rows_tok = self.load_rows(["bq", "bk", "bv", "bo", "b2"])
            RS = dict(self.ROW_SLOT)

            XTC = [sbuf("A_XTC0", (128, 8, 512), BF16), sbuf("A_XTC1", (128, 8, 512), BF16)]
            st2 = contextlib.ExitStack()
            sbuf2 = lambda n, sh, dty: self._sbuf(n, sh, dty, st2)
            WB = [sbuf2("A_WB0", (128, 8, D), BF16), sbuf2("A_WB1", (128, 8, D), BF16)]
            QNR = sbuf2("A_QNR", (128, 2, 512), BF16)
            KNR = sbuf2("A_KNR", (128, 2, D), BF16)
            VBR = sbuf2("A_VBR", (128, 2, D), BF16)
            CTXC = [sbuf2("A_CTXC0", (128, 8, 512), BF16)]

            ctok = (self.tok_const, rows_tok)
            wk_tok = p.dma("G", WB[0][:], dt["Wk"])
            wv_tok = p.dma("G", WB[1][:], dt["Wv"])
            kv_last = None
            for tc in range(4):
                cur = XTC[tc % 2]
                tr_tok = self.transpose_into(
                    lambda tb, fc: self.XR[:, tb, ts(fc)], cur, 0, ps_t, x_dep,
                    src_tb_base=tc * 4)
                for pr in range(2):
                    ek_toks = {}
                    for u in range(2):
                        tt = pr * 2 + u
                        tb = tc * 4 + tt
                        sl = tb % 2
                        pk = ps_o[sl]

                        def kmm(e, tt=tt, pk=pk, cur=cur):
                            for hf in range(2):
                                for kc in range(NF):
                                    e.matmul(pk[:, ts(hf, 512)], cur[:, kc, ts(tt)],
                                             WB[0][:, kc, ts(hf, 512)],
                                             start=(kc == 0), stop=False)
                            return self.rank1(e, pk[:], RS["bk"])
                        mm = p.step("T", kmm, deps=[tr_tok, wk_tok, ctok,
                                                    self.free.get(("pso", sl))])
                        t2 = p.step("S", lambda e, pk=pk, sl=sl: e.activation(
                            TMPB[:, sl, :], pk[:], AF.Exp), deps=[mm])
                        m1 = p.step("V", lambda e, sl=sl: e.tensor_scalar(
                            TMP[:, sl, :], TMPB[:, sl, :], 1.0, -1.0, ALU.min, ALU.add),
                            deps=[t2])
                        ek = p.step("V", lambda e, pk=pk, sl=sl: e.scalar_tensor_tensor(
                            TMP[:, sl, :], pk[:], 0.0, TMP[:, sl, :], ALU.max, ALU.add),
                            deps=[m1])
                        self.free[("pso", sl)] = ek
                        ek_toks[sl] = ek
                        sq = p.step("V", lambda e, sl=sl: e.tensor_tensor(
                            SCR[:].rearrange("p (h d) -> p h d", d=64),
                            TMP[:, sl, :].rearrange("p (h d) -> p h d", d=64),
                            TMP[:, sl, :].rearrange("p (h d) -> p h d", d=64),
                            ALU.mult), deps=[ek])
                        rd = p.step("V", lambda e, sl=sl: e.tensor_reduce(
                            self.SMALL[:, 16 * sl:16 * sl + 16],
                            SCR[:].rearrange("p (h d) -> p h d", d=64),
                            AX.X, ALU.add), deps=[sq])
                    rcp = p.step("V", lambda e: e.reciprocal(
                        self.SMALL[:, 32:64], self.SMALL[:, 0:32]), deps=[rd])
                    rsq = p.step("S", lambda e: e.activation(
                        self.SMALL[:, 0:32], self.SMALL[:, 32:64], AF.Sqrt), deps=[rcp])
                    pend = []
                    for u in range(2):
                        tt = pr * 2 + u
                        tb = tc * 4 + tt
                        sl = tb % 2
                        kn_t = p.step("V", lambda e, sl=sl: e.tensor_tensor(
                            KNR[:, sl, :].rearrange("p (h d) -> p h d", d=64),
                            TMP[:, sl, :].rearrange("p (h d) -> p h d", d=64),
                            self.SMALL[:, 16 * sl:16 * sl + 16].unsqueeze(2)
                            .broadcast_to([128, 16, 64]),
                            ALU.mult), deps=[rsq, ek_toks[sl],
                                             self.free.get(("knr", sl))])
                        pv = ps_o[(tb + 1) % 2]

                        def vmm(e, tt=tt, pv=pv, cur=cur):
                            for hf in range(2):
                                for kc in range(NF):
                                    e.matmul(pv[:, ts(hf, 512)], cur[:, kc, ts(tt)],
                                             WB[1][:, kc, ts(hf, 512)],
                                             start=(kc == 0), stop=False)
                            return self.rank1(e, pv[:], RS["bv"])
                        mmv = p.step("T", vmm, deps=[wv_tok, ctok,
                                                     self.free.get(("pso", (tb + 1) % 2))])
                        vev = p.step("V", lambda e, pv=pv, sl=sl: e.tensor_copy(
                            VBR[:, sl, :], pv[:]), deps=[mmv, self.free.get(("vbr", sl))])
                        self.free[("pso", (tb + 1) % 2)] = vev
                        pend.append((tb, sl, kn_t, vev))
                    for tb, sl, kn_t, vev in pend:
                        def kvmm(e, tb=tb, sl=sl):
                            last = None
                            for h in range(16):
                                a = h % 2
                                last = e.matmul(
                                    ps_kv[ts(a, 64), (h // 2) * 64:(h // 2) * 64 + 64],
                                    KNR[:, sl, ts(h, 64)], VBR[:, sl, ts(h, 64)],
                                    start=(tb == 0 and h <= 1),
                                    stop=(tb == NT - 1 and h >= 14),
                                    tile_position=(0, 64 * a), skip_group_check=True)
                            return last
                        kv_last = p.step("T", kvmm, deps=[kn_t, vev])
                        self.free[("vbr", sl)] = kv_last
                        self.free[("knr", sl)] = kv_last
            KVS = KNR
            kv_ev = p.step("V", lambda e: e.tensor_scalar(
                KVS[:, 0, 0:512], ps_kv[:], 0.125, None, ALU.mult), deps=[kv_last])
            wq_tok = p.dma("G", WB[0][:], dt["Wq"], deps=[kv_last])
            wo_tok = p.dma("G", WB[1][:], dt["Wo"], deps=[kv_last])
            s1_toks = [None] * NT
            for tc in range(4):
                cur = XTC[tc % 2]
                ctxc = CTXC[0]
                tr_tok = self.transpose_into(
                    lambda tb, fc: self.XR[:, tb, ts(fc)], cur, 0, ps_t,
                    x_dep + [kv_last], src_tb_base=tc * 4)
                ctx_done = []
                for fp in range(4):
                    eq_toks = {}
                    for u in range(2):
                        ft = fp * 2 + u
                        sl = ft % 2
                        pq = ps_q[sl]

                        def qmm(e, ft=ft, pq=pq, cur=cur):
                            for kc in range(NF):
                                e.matmul(pq[:], WB[0][:, kc, ts(ft)], cur[:, kc, :],
                                         start=(kc == 0), stop=False)
                            return self.rank1_fm(e, pq[:], RS["bq"], ft, 512)
                        mm = p.step("T", qmm, deps=[tr_tok, wq_tok, ctok,
                                                    self.free.get(("psq", sl))])
                        t2 = p.step("S", lambda e, pq=pq, sl=sl: e.activation(
                            TMPB[:, sl, 0:512], pq[:], AF.Exp), deps=[mm])
                        m1 = p.step("V", lambda e, sl=sl: e.tensor_scalar(
                            TMP[:, sl, 0:512], TMPB[:, sl, 0:512], 1.0, -1.0,
                            ALU.min, ALU.add), deps=[t2])
                        eq = p.step("V", lambda e, pq=pq, sl=sl: e.scalar_tensor_tensor(
                            TMP[:, sl, 512:1024], pq[:], 0.0, TMP[:, sl, 0:512],
                            ALU.max, ALU.add), deps=[m1])
                        self.free[("psq", sl)] = eq
                        eq_toks[sl] = eq
                        sq = p.step("V", lambda e, sl=sl: e.tensor_tensor(
                            TMP[:, sl, 0:512], TMP[:, sl, 512:1024], TMP[:, sl, 512:1024],
                            ALU.mult), deps=[eq])
                    for u in range(2):
                        ft = fp * 2 + u
                        sl = ft % 2
                        pq = ps_q[sl]
                        ssmm = p.step("T", lambda e, sl=sl: e.matmul(
                            ps_ss[0:2, :], self.BO2[:], TMP[:, sl, 0:512],
                            start=True, stop=True),
                            deps=[sq, self.free.get("ps_t")])
                        rc = p.step("V", lambda e, sl=sl: e.reciprocal(
                            SROW[0:2, 0:512], ps_ss[0:2, :]),
                            deps=[ssmm, self.free.get("srow")])
                        rq = p.step("S", lambda e, sl=sl: e.activation(
                            SROWB[0:2, 0:512], SROW[0:2, 0:512], AF.Sqrt),
                            deps=[rc, self.free.get("srowb")])
                        self.free["srow"] = rq
                        bc = p.step("T", lambda e, sl=sl: e.matmul(
                            ps_ss[:, :], self.SEL2[:], SROWB[0:2, 0:512],
                            start=True, stop=True),
                            deps=[rq])
                        self.free["srowb"] = bc
                        qn_t = p.step("V", lambda e, sl=sl: e.tensor_tensor(
                            QNR[:, sl, :], TMP[:, sl, 512:1024], ps_ss[:, :], ALU.mult),
                            deps=[bc, eq_toks[sl], self.free.get(("qnr", sl))])
                        self.free["ps_t"] = qn_t

                        def cmm(e, ft=ft, pq=pq, sl=sl):
                            last = None
                            for a in range(2):
                                h = 2 * ft + a
                                last = e.matmul(
                                    pq[ts(a, 64), :],
                                    KVS[ts(a, 64), 0, (h // 2) * 64:(h // 2) * 64 + 64],
                                    QNR[ts(a, 64), sl, :],
                                    start=True, stop=True,
                                    tile_position=(64 * a, 64 * a), skip_group_check=True)
                            return last
                        mm2 = p.step("T", cmm, deps=[kv_ev, qn_t])
                        ev = p.step("V", lambda e, ft=ft, pq=pq, ctxc=ctxc: e.tensor_copy(
                            ctxc[:, ft, :], pq[:]), deps=[mm2, self.free.get(("ctxc", ft))])
                        self.free[("psq", sl)] = ev
                        self.free[("qnr", sl)] = mm2
                        ctx_done.append(ev)
                for tt in range(4):
                    tb = tc * 4 + tt
                    po = ps_o[tt % 2]

                    def omm(e, tt=tt, po=po, ctxc=ctxc):
                        for hf in range(2):
                            for kc in range(NF):
                                e.matmul(po[:, ts(hf, 512)], ctxc[:, kc, ts(tt)],
                                         WB[1][:, kc, ts(hf, 512)],
                                         start=(kc == 0), stop=False)
                        return self.rank1(e, po[:], RS["bo"])
                    mm = p.step("T", omm, deps=ctx_done + [wo_tok, ctok,
                                                           self.free.get(("pso", tt % 2))])
                    ev = p.step("V", lambda e, tb=tb, po=po: e.scalar_tensor_tensor(
                        self.XR[:, tb, :], po[:], 1.0, self.XR[:, tb, :],
                        ALU.bypass, ALU.add), deps=[mm, x0b_toks[tb]])
                    self.free[("pso", tt % 2)] = ev
                    s1_toks[tb] = ev
                for ft in range(NF):
                    self.free[("ctxc", ft)] = s1_toks[tc * 4 + 3]
            self.dump_xr("a_s1", s1_toks)
            st2.close()
            BCT = sbuf("A_BCT", (128, 4, D), F32)
            bct_toks = [p.dma("Y", BCT[:, i, :], dt[n + "_bc"], deps=s1_toks)
                        for n, i in [("g1", 0), ("be1", 1), ("g2", 2), ("be2", 3)]]
            a_toks = self.layernorm(self.XR, s1_toks, 1e-12,
                                    gamma=BCT[:, 0, :], beta=BCT[:, 1, :],
                                    extra_deps=bct_toks)
            self.dump_xr("a_attn", a_toks)
            ATC = XTC
            HC4 = sbuf("A_HC4", (128, 4, 512), BF16)
            W1G = sbuf("A_W1G", (128, 4, 8, 128), BF16)
            W2S = [sbuf("A_W2S0", (128, 4, D), BF16), sbuf("A_W2S1", (128, 4, D), BF16)]
            s2_toks = [None] * NT
            for tc in range(4):
                atc = ATC[tc % 2]
                at_tok = self.transpose_into(
                    lambda tb, fc: self.XR[:, tb, ts(fc)], atc, 0, ps_t,
                    a_toks[tc * 4:tc * 4 + 4] + [kv_last], src_tb_base=tc * 4)
                for g in range(8):
                    w2tok = p.dma("G", W2S[g % 2][:],
                                  dt["W2"][:, g * 4:(g + 1) * 4, :],
                                  deps=[self.free.get(("w2s", g % 2), s1_toks[-1])])
                    w1tok = p.dma("G", W1G[:],
                                  dt["W1f"][g * 4:(g + 1) * 4].rearrange("g p c f -> p g c f"),
                                  deps=[self.free.get("w1g", s1_toks[-1])])
                    h_done = []
                    for fi in range(4):
                        ft = g * 4 + fi
                        pq = ps_q[fi % 2]

                        def hmm(e, fi=fi, pq=pq, atc=atc):
                            last = None
                            for kc in range(NF):
                                last = e.matmul(pq[:], W1G[:, fi, kc, :], atc[:, kc, :],
                                                start=(kc == 0), stop=(kc == NF - 1))
                            return last
                        mm = p.step("T", hmm, deps=[at_tok, w1tok,
                                                    self.free.get(("psq", fi % 2))])
                        hv = p.step("S", lambda e, fi=fi, pq=pq, ft=ft: e.activation(
                            HC4[:, fi, :], pq[:], AF.Gelu, bias=self.BPP[:, ft:ft + 1]),
                            deps=[mm, self.tok_const, self.free.get(("hc4", fi))])
                        self.free[("psq", fi % 2)] = hv
                        h_done.append(hv)
                    self.free["w1g"] = h_done[-1]
                    for tt in range(4):
                        tb = tc * 4 + tt
                        po = ps_o[tt % 2]

                        def w2mm(e, g=g, tt=tt, po=po):
                            last = None
                            for hf in range(2):
                                for fi in range(4):
                                    last = e.matmul(po[:, ts(hf, 512)], HC4[:, fi, ts(tt)],
                                                    W2S[g % 2][:, fi, ts(hf, 512)],
                                                    start=(fi == 0), stop=(fi == 3 and g != 0))
                            if g == 0:
                                return self.rank1(e, po[:], RS["b2"])
                            return last
                        mm = p.step("T", w2mm, deps=h_done + [w2tok,
                                                              self.free.get(("pso", tt % 2))])
                        ev = p.step("V", lambda e, tb=tb, po=po: e.tensor_tensor(
                            self.XR[:, tb, :], po[:], self.XR[:, tb, :], ALU.add),
                            deps=[mm, s2_toks[tb] or a_toks[tb]])
                        self.free[("pso", tt % 2)] = ev
                        s2_toks[tb] = ev
                    for fi in range(4):
                        self.free[("hc4", fi)] = s2_toks[tc * 4 + 3]
                    self.free[("w2s", g % 2)] = s2_toks[tc * 4 + 3]
            self.dump_xr("a_s2", s2_toks)
            s3_toks = self.layernorm(self.XR, s2_toks, 1e-12,
                                     gamma=BCT[:, 2, :], beta=BCT[:, 3, :],
                                     add_src=self.X0B, extra_deps=bct_toks)
            self.dump_xr("a_s3", s3_toks)
            x1_toks = self.layernorm(self.XR, s3_toks, 1e-5)
            return x1_toks

    def emit_phaseB(self, toks):
        self._phase_reset()
        """XR holds x1 -> x2 (expert-choice MoE + module LN)."""
        p, dt, nc = self.p, self.dt, self.nc
        with contextlib.ExitStack() as st:
            def psum(name, shape):
                return st.enter_context(nc.psum_tensor(self._uniq(name), list(shape), F32))
            sbuf = lambda n, sh, dty: self._sbuf(n, sh, dty, st)

            ps_t = psum("B_pst", (128, 512))
            ps_q = [psum("B_psq0", (128, 512)), psum("B_psq1", (128, 512))]
            ps_o = [psum("B_pso0", (128, 1024)), psum("B_pso1", (128, 1024))]
            ps_tbB = st.enter_context(nc.psum_tensor(
                self._uniq("B_pstb"), [128, 512], BF16))
            ps_r = ps_q[0][:, 0:8]

            TMP, SCR = self.TMP, self.SCR
            x_dep = list(toks) if isinstance(toks, (list, tuple)) else [toks]
            GIDALL = sbuf("B_GIDALL", (128, 16, 8, 2), F32)
            POSI32 = sbuf("B_POSI", (128, 16, 8), I32)

            with contextlib.ExitStack() as st2:
                sbuf2 = lambda n, sh, dty: self._sbuf(n, sh, dty, st2)
                if self._cur_batch == 0:
                    CH = [sbuf2("MC_CH0", (128, 4096), BF16),
                          sbuf2("MC_CH1", (128, 4096), BF16)]
                    self._emit_moe_cast(CH)
                X1TF = sbuf2("B_X1TF", (128, 8, 512), F32)
                CW = sbuf2("B_CW", (128, 8, 8), F32)
                cw_tok = p.dma("Y", CW[:], dt["choice_wT"])
                PR = sbuf2("B_PR", (128, 16, 8), F32)
                GT = sbuf2("B_GT", (128, 16, 8), F32)
                PT8 = sbuf2("B_PT8", (8, 2048), F32)
                BMS = sbuf2("B_BMS", (8, 2048), BF16)
                BV = sbuf2("B_BV", (8, 8), F32)
                THRB = sbuf2("B_THRB", (128, 8), F32)
                CSF = sbuf2("B_CSF", (8, 2048), F32)
                POST = sbuf2("B_POST", (128, 16, 8), F32)
                ZB = sbuf2("B_ZB", (128, 1024), F32)
                xcp = [p.step("V", lambda e, tb=tb: e.tensor_copy(
                    self.X0B[:, tb, :], self.XR[:, tb, :]), deps=x_dep)
                    for tb in range(NT)]
                pr_toks = []
                for tc in range(4):
                    ev32 = None
                    for fc in range(NF):
                        tk = p.step("T", lambda e, fc=fc, tc=tc: [
                            e.transpose(ps_t[:, ts(tt)], self.XR[:, tc * 4 + tt, ts(fc)],
                                        self.IDT[:]) for tt in range(4)][-1],
                            deps=x_dep + [self.free.get("ps_t")])
                        ev32 = p.step("V", lambda e, fc=fc: e.tensor_copy(
                            X1TF[:, fc, :], ps_t[:]), deps=[tk, self.free.get(("x1tf", fc))])
                        self.free["ps_t"] = ev32
                    for tt in range(4):
                        tb = tc * 4 + tt

                        def rmm(e, tt=tt):
                            last = None
                            for kc in range(NF):
                                last = e.matmul(ps_r, X1TF[:, kc, ts(tt)], CW[:, kc, :],
                                                start=(kc == 0), stop=(kc == NF - 1))
                            return last
                        mm = p.step("T", rmm, deps=[ev32, cw_tok, self.free.get("psr")])
                        nm = p.step("V", lambda e: e.tensor_reduce(
                            self.SMALL[:, 0:1], ps_r, AX.X, ALU.max, negate=True),
                            deps=[mm])
                        ex = p.step("S", lambda e, tb=tb: e.activation(
                            PR[:, tb, :], ps_r, AF.Exp, bias=self.SMALL[:, 0:1],
                            accum_out=self.SMALL[:, 1:2]), deps=[nm])
                        rz = p.step("V", lambda e: e.reciprocal(
                            self.SMALL[:, 2:3], self.SMALL[:, 1:2]), deps=[ex])
                        pn = p.step("V", lambda e, tb=tb: e.tensor_scalar(
                            PR[:, tb, :], PR[:, tb, :], self.SMALL[:, 2:3], None, ALU.mult),
                            deps=[rz])
                        self.free["psr"] = pn
                        pr_toks.append(pn)
                    for fc in range(NF):
                        self.free[("x1tf", fc)] = pr_toks[-1]
                pt_toks = []
                for tg in range(4):
                    tk = p.step("T", lambda e, tg=tg: [
                        e.transpose(ps_t[0:8, ts(tt)], PR[:, tg * 4 + tt, :], self.IDT[:])
                        for tt in range(4)][-1],
                        deps=pr_toks[tg * 4:tg * 4 + 4] + [self.free.get("ps_t")])
                    ev = p.step("V", lambda e, tg=tg: e.tensor_copy(
                        PT8[0:8, ts(tg, 512)], ps_t[0:8, :]), deps=[tk])
                    self.free["ps_t"] = ev
                    pt_toks.append(ev)
                t = p.step("V", lambda e: e.memset(BV[:, 0:1], 0.0), deps=pt_toks)
                t = p.step("V", lambda e: e.memset(BV[:, 1:2], 1.0), deps=[t])
                for it in range(36):
                    t = p.step("V", lambda e: e.tensor_tensor(
                        BV[:, 2:3], BV[:, 0:1], BV[:, 1:2], ALU.add), deps=[t])
                    t = p.step("V", lambda e: e.tensor_scalar(
                        BV[:, 2:3], BV[:, 2:3], 0.5, None, ALU.mult), deps=[t])
                    t = p.step("V", lambda e: e.tensor_scalar(
                        BMS[0:8, :], PT8[0:8, :], BV[0:8, 2:3], 0.0, ALU.is_gt, ALU.add,
                        accum_out=BV[:, 3:4]), deps=[t])
                    t = p.step("V", lambda e: e.tensor_scalar(
                        BV[:, 4:5], BV[:, 3:4], 512.0, None, ALU.is_ge), deps=[t])
                    t = p.step("V", lambda e: e.scalar_tensor_tensor(
                        BV[:, 5:6], BV[:, 2:3], 1.0, BV[:, 0:1], ALU.bypass, ALU.subtract),
                        deps=[t])
                    t = p.step("V", lambda e: e.tensor_tensor(
                        BV[:, 5:6], BV[:, 5:6], BV[:, 4:5], ALU.mult), deps=[t])
                    t = p.step("V", lambda e: e.tensor_tensor(
                        BV[:, 0:1], BV[:, 0:1], BV[:, 5:6], ALU.add), deps=[t])
                    t = p.step("V", lambda e: e.scalar_tensor_tensor(
                        BV[:, 5:6], BV[:, 1:2], 1.0, BV[:, 2:3], ALU.bypass, ALU.subtract),
                        deps=[t])
                    t = p.step("V", lambda e: e.tensor_tensor(
                        BV[:, 5:6], BV[:, 5:6], BV[:, 4:5], ALU.mult), deps=[t])
                    t = p.step("V", lambda e: e.scalar_tensor_tensor(
                        BV[:, 1:2], BV[:, 5:6], 1.0, BV[:, 2:3], ALU.bypass, ALU.add),
                        deps=[t])
                bd1 = p.dma("G", dt["thr_scr"].transpose([1, 0]), BV[0:8, 0:1],
                            deps=[t], slow=True)
                bt = p.dma("G", THRB[:], dt["thr_scr"].broadcast_to([128, 8]), deps=[bd1])
                gt_toks = []
                for tb in range(NT):
                    g1 = p.step("V", lambda e, tb=tb: e.tensor_tensor(
                        SCR[:, 0:8], PR[:, tb, :], THRB[:], ALU.is_gt), deps=[bt])
                    g2 = p.step("V", lambda e, tb=tb: e.tensor_tensor(
                        GT[:, tb, :], PR[:, tb, :], SCR[:, 0:8], ALU.mult), deps=[g1])
                    gt_toks.append(g2)
                POSF = sbuf2("B_POSF", (128, 16, 8), F32)
                PCF = sbuf2("B_PCF", (128, 1), F32)
                zb0 = p.step("V", lambda e: e.memset(ZB[:], 0.0), deps=x_dep)
                z_toks = [p.dma("Y", dt["y_scr"].rearrange(
                    "(t p) d -> p t d", p=128)[:, tb, :], ZB[:], deps=[zb0])
                    for tb in range(NT)]
                pc_tok = p.dma("Y", PCF[:], dt["pcol"])
                gidi = p.step("V", lambda e: e.tensor_copy(
                    GIDALL[:, :, :, 0], GT[:]), deps=gt_toks)
                gidt = None
                for tb in range(NT):
                    gidt = p.step("V", lambda e, tb=tb: e.tensor_scalar(
                        GIDALL[:, tb, :, 1], PCF[:, 0:1].broadcast_to([128, 8]),
                        float(tb * 128), None, ALU.add),
                        deps=[pc_tok, gidi])
                mk = p.step("V", lambda e: e.tensor_scalar(
                    CSF[0:8, :], PT8[0:8, :], BV[0:8, 0:1], None, ALU.is_gt),
                    deps=[bd1, pt_toks])
                t = p.step("V", lambda e: e.tensor_tensor_scan(
                    PT8[0:8, 0:1024], CSF[0:8, 0:1024],
                    CSF[0:8, 0:1024], 0.0, ALU.add, ALU.bypass), deps=[mk])
                t = p.step("V", lambda e: e.tensor_tensor_scan(
                    PT8[0:8, 1024:2048], CSF[0:8, 1024:2048],
                    CSF[0:8, 1024:2048], 0.0, ALU.add, ALU.bypass), deps=[t])
                t = p.step("V", lambda e: e.scalar_tensor_tensor(
                    PT8[0:8, 1024:2048],
                    PT8[0:8, 1023:1024].broadcast_to([8, 1024]), 1.0,
                    PT8[0:8, 1024:2048], ALU.bypass, ALU.add), deps=[t])
                t = p.step("V", lambda e: e.tensor_scalar(
                    PT8[0:8, :], PT8[0:8, :], 999999.0, None, ALU.add), deps=[t])
                pos2 = p.step("V", lambda e: e.scalar_tensor_tensor(
                    CSF[0:8, :], CSF[0:8, :], -1000000.0, PT8[0:8, :],
                    ALU.mult, ALU.add), deps=[t])
                ptt_toks = []
                for tb in range(NT):
                    tk = p.step("T", lambda e, tb=tb: e.transpose(
                        ps_t[:, 0:8], CSF[0:8, ts(tb)], self.IDT[0:8, 0:8]),
                        deps=[pos2, self.free.get("ps_t")])
                    ev = p.step("V", lambda e, tb=tb: e.tensor_copy(
                        POSF[:, tb, :], ps_t[:, 0:8]), deps=[tk])
                    self.free["ps_t"] = ev
                    ptt_toks.append(ev)
                pi = p.step("V", lambda e: e.tensor_copy(POSI32[:], POSF[:]),
                            deps=ptt_toks)
                BREG = self.__dict__.setdefault("_breg", {})

                def mkreg(e):
                    if "r" not in BREG:
                        BREG["r"] = e.alloc_register("bnd511")
                    return e.reg_mov(BREG["r"], 511)
                rtok = p.step("G", mkreg, deps=[])
                IOA = bass.IndirectOffsetOnAxis
                xs_flat = dt["xs_scr"].rearrange("e s d -> (e s) d")
                gid_flat = dt["gid_scr"].rearrange("e s d -> (e s) d")
            W1H = sbuf("B_W1H", (128, 8, 8, 128), BF16)
            W2H = sbuf("B_W2H", (128, 16, 512), BF16)
            HE = sbuf("B_HE", (128, 16, 512), BF16)
            XSEL = sbuf("B_XSEL", (128, 4, 1024), BF16)
            XSELF = sbuf("B_XSELF", (128, 8, 512), BF16)
            YS = sbuf("B_YS", (128, 8, 512), F32)
            GIDL = sbuf("B_GIDL", (128, 4, 2), F32)
            ID32 = sbuf("B_ID32", (128, 4), I32)
            ya_last = None
            for e_i in range(8):
                sct = None
                for tb in range(NT):
                    sct = p.dma_fn("G", lambda e, e_i=e_i, tb=tb:
                        e.indirect_dma_start(
                            xs_flat, IOA(POSI32[:, tb, e_i:e_i + 1], 0),
                            self.X0B[:, tb, :], None, bounds_check=BREG["r"],
                            oob_is_err=False,
                            element_offset=e_i * 512 * 1024),
                        deps=[xcp[tb], pi, rtok], chain=(tb == 0))
                git = None
                for tb in range(NT):
                    git = p.dma_fn("G", lambda e, e_i=e_i, tb=tb:
                        e.indirect_dma_start(
                            gid_flat, IOA(POSI32[:, tb, e_i:e_i + 1], 0),
                            GIDALL[:, tb, e_i, :], None, bounds_check=BREG["r"],
                            oob_is_err=False,
                            element_offset=e_i * 512 * 2),
                        deps=[gidt, pi, rtok], chain=(tb == 0))
                xl = p.dma("Y", XSEL[:], dt["xs_scr"][e_i].rearrange(
                    "(c p) d -> p c d", p=128),
                    deps=[git, self.free.get("xsel")])
                gl = p.dma("Y", GIDL[:], dt["gid_scr"][e_i].rearrange(
                    "(c p) d -> p c d", p=128),
                    deps=[git, self.free.get("gidl")])
                idt = p.step("V", lambda e: e.tensor_copy(
                    ID32[:], GIDL[:, :, 1]), deps=[gl, self.free.get("id32")])
                tr_toks = []
                for kc in range(NF):
                    tk = p.step("T", lambda e, kc=kc: [
                        e.transpose(ps_tbB[:, ts(t)], XSEL[:, t, ts(kc)],
                                    self.IDTB[:]) for t in range(4)][-1],
                        deps=[xl, self.free.get("pstbB")])
                    evt = p.step("V", lambda e, kc=kc: e.tensor_copy(
                        XSELF[:, kc, :], ps_tbB[:]), deps=[tk])
                    self.free["pstbB"] = evt
                    tr_toks.append(evt)
                self.free["xsel"] = tr_toks[-1]
                h_done = []
                for hh in range(2):
                    w1tok = p.dma("Y", W1H[:],
                                  dt["moe_w1b"][e_i][:, 8 * hh:8 * hh + 8],
                                  deps=[self.free.get("w1h")])
                    for lf in range(8):
                        lft = hh * 8 + lf
                        pq = ps_q[lft % 2]

                        def hmm(e, lf=lf, pq=pq):
                            last = None
                            for kc in range(NF):
                                last = e.matmul(pq[:], W1H[:, lf, kc, :],
                                                XSELF[:, kc, :],
                                                start=(kc == 0), stop=(kc == NF - 1))
                            return last
                        mm = p.step("T", hmm, deps=[w1tok, tr_toks[-1:],
                                                    self.free.get(("psq", lft % 2))])
                        if lf == 7:
                            self.free["w1h"] = mm
                        hv = p.step("S", lambda e, lft=lft, pq=pq: e.activation(
                            HE[:, lft, :], pq[:], AF.Silu),
                            deps=[mm, self.free.get(("he", lft))])
                        self.free[("psq", lft % 2)] = hv
                        h_done.append(hv)
                for hf in range(2):
                    w2tok = p.dma("Y", W2H[:],
                                  dt["moe_w2b"][e_i][:, :, 512 * hf:512 * hf + 512],
                                  deps=[self.free.get("w2h")])
                    for tt in range(4):
                        po = ps_o[tt % 2]

                        def ymm(e, tt=tt, po=po, hf=hf):
                            last = None
                            for fc in range(16):
                                last = e.matmul(po[:, ts(hf, 512)], HE[:, fc, ts(tt)],
                                                W2H[:, fc, :],
                                                start=(fc == 0), stop=(fc == 15))
                            return last
                        mm = p.step("T", ymm, deps=h_done + [w2tok,
                                                             self.free.get(("pso", tt % 2, hf))])
                        if tt == 3:
                            self.free["w2h"] = mm
                            if hf == 1:
                                for lft in range(16):
                                    self.free[("he", lft)] = mm
                        par = tt * 2 + hf
                        ev = p.step("V", lambda e, tt=tt, po=po, hf=hf, par=par:
                            e.tensor_scalar(
                                YS[:, par, :], po[:, ts(hf, 512)], GIDL[:, tt, 0:1],
                                None, ALU.mult),
                            deps=[mm, gl, self.free.get("ya_grp")])
                        self.free[("pso", tt % 2, hf)] = ev
                        ya = p.dma_fn("G", lambda e, tt=tt, hf=hf, par=par:
                            e.indirect_dma_start(
                                dt["y_scr"], IOA(ID32[:, tt:tt + 1], 0), YS[:, par, :],
                                None, compute_op=ALU.add,
                                element_offset=512 * hf),
                            deps=[ev, idt] + ([z_toks] if e_i == 0 else []),
                            chain=(tt == 0 and hf == 0))
                        ya_last = ya
                self.free["ya_grp"] = ya_last
                self.free["id32"] = ya_last
                self.free["gidl"] = ya_last
            acc_toks = []
            for tb in range(NT):
                par = (tb % 4) * 2
                yl = p.dma("Y", YS[:, par:par + 2, :].rearrange("p a w -> p (a w)"),
                           dt["y_scr"].rearrange(
                    "(t p) d -> p t d", p=128)[:, tb, :],
                    deps=[ya_last, self.free.get(("yl", par))])
                av = p.step("V", lambda e, tb=tb, par=par:
                    e.scalar_tensor_tensor(
                        self.XR[:, tb, :],
                        YS[:, par:par + 2, :].rearrange("p a w -> p (a w)"), 1.0,
                        self.XR[:, tb, :],
                        ALU.bypass, ALU.add), deps=[yl])
                self.free[("yl", par)] = av
                acc_toks.append(av)
            x2_toks = self.layernorm(self.XR, acc_toks, 1e-5)
            return x2_toks

    def emit_phaseC(self, toks):
        self._phase_reset()
        """XR holds x2 -> x3 (causal ALiBi attention + module LN)."""
        p, dt, nc = self.p, self.dt, self.nc
        with contextlib.ExitStack() as st:
            def psum(name, shape, dty=F32):
                return st.enter_context(nc.psum_tensor(self._uniq(name), list(shape), dty))
            sbuf = lambda n, sh, dty: self._sbuf(n, sh, dty, st)

            ps_s = psum("C_pss", (128, 2048))
            ps_a = psum("C_psa", (128, 512))
            ps_t = ps_a[:]
            ps_tb = psum("C_pstb", (128, 512), BF16)
            ps_tb2 = psum("C_pstb2", (128, 512), BF16)
            PSB = [ps_tb, ps_tb2]
            ps_pv = psum("C_pspv", (128, 512))
            ps_pj = ps_a

            x_dep = list(toks) if isinstance(toks, (list, tuple)) else [toks]
            self._c_pend = None
            xtv = self.X0B[:].rearrange("p t d -> p (t d)").rearrange(
                "p (c w) -> p c w", c=8)
            xt_toks = [self.transpose_into(
                lambda tb, fc: self.XR[:, tb, ts(fc)], xtv, tc * 512, ps_a,
                x_dep, bank="ps_a", src_tb_base=tc * 4) for tc in range(4)]
            x0b_toks = xt_toks
            rows_tok = self.load_rows(["bv1", "bproj"])
            RS = dict(self.ROW_SLOT)
            bqs = p.step("V", lambda e: e.tensor_scalar(
                self.BPP[:, 80:88], self.BPP[:, 40:48], 0.125, None, ALU.mult),
                deps=[self.tok_const])

            MMASK = sbuf("C_MMASK", (128, 128), F32)
            SB32 = sbuf("C_SB32", (128, 2048), F32)
            J4 = sbuf("C_J4", (128, S), BF16)
            cm_toks = (p.dma("Y", MMASK[:], dt["mmask"]),
                       p.dma("G", J4[:], dt["j4pack"]))
            KTG = sbuf("C_KTG", (128, 4, S), BF16)
            QTG = sbuf("C_QTG", (128, 4, S), BF16)
            VG = sbuf("C_VG", (128, NT, 512), BF16)
            WPB = sbuf("C_WPB", (128, 4, 1024), BF16)
            _xtc = sbuf("C_XTC0", (128, 8, 512), BF16)
            XTC = [_xtc, _xtc]
            _xf = _xtc[:].rearrange("p c w -> p (c w)")
            PB = _xf[:, 0:2048]
            PTB = [_xf[:, 2048:2560], _xf[:, 3072:3584]]
            CTC = _xf[:, 2560:3072].rearrange("p (c w) -> p c w", c=4)
            _wkb = sbuf("C_WKB0", (128, 8, 128), BF16)
            WKB = [_wkb, _wkb]
            WVB = sbuf("C_WVB", (128, 8, 256), BF16)

            prj_toks = [xt_toks[-1]] * NT
            for hg in range(2):
                def load_wv(vh):
                    return p.dma("G", WVB[:], dt["Wv1"]
                        [:, :, 512 * hg + 256 * vh:512 * hg + 256 * (vh + 1)],
                        deps=[self.free.get("wvb")])
                kq_done = []
                for tc in range(4):
                    tr_tok = xt_toks[tc]
                    cur = xtv[:, :, ts(tc, 512)].rearrange("p c w -> p c w")
                    for src, dstT, bias_col, blk0 in (
                            ("k", KTG, 32, 4 * hg), ("q", QTG, 56, 8 + 4 * hg)):
                        for lft in range(4):
                            wtok = p.dma("G", WKB[lft % 2][:],
                                         dt["Wkqv_f"][blk0 + lft],
                                         deps=[self.free.get("wkb")])
                            def pmm(e, lft=lft, cur=cur):
                                last = None
                                for kc in range(NF):
                                    last = e.matmul(ps_t, WKB[lft % 2][:, kc, :],
                                                    cur[:, kc, :],
                                                    start=(kc == 0), stop=(kc == NF - 1))
                                return last
                            mm = p.step("T", pmm, deps=[tr_tok, wtok,
                                                        self.free.get("ps_a")])
                            if src == "k":
                                ev = p.step("S", lambda e, lft=lft, tc=tc, dstT=dstT, bias_col=bias_col, hg=hg: e.activation(
                                    dstT[:, lft, ts(tc, 512)], ps_t, AF.Identity,
                                    bias=self.BPP[:, bias_col + 4 * hg + lft:bias_col + 4 * hg + lft + 1]),
                                    deps=[mm])
                            else:
                                ev = p.step("S", lambda e, lft=lft, tc=tc, dstT=dstT, hg=hg: e.activation(
                                    dstT[:, lft, ts(tc, 512)], ps_t, AF.Identity,
                                    bias=self.BPP[:, 80 + 4 * hg + lft:81 + 4 * hg + lft],
                                    scale=0.125),
                                    deps=[mm, bqs])
                            self.free["wkb"] = ev
                            self.free["ps_a"] = ev
                            kq_done.append(ev)
                v_done = []
                for vh in range(2):
                    wv_tok = load_wv(vh)
                    for tc in range(4):
                        tr_tok = xt_toks[tc]
                        cur = xtv[:, :, ts(tc, 512)].rearrange("p c w -> p c w")
                        for tt in range(4):
                            tb = tc * 4 + tt

                            def vmm(e, tt=tt, cur=cur, vh=vh, hg=hg):
                                for kc in range(NF):
                                    e.matmul(ps_a[:, 0:256], cur[:, kc, ts(tt)],
                                             WVB[:, kc, :], start=(kc == 0), stop=False)
                                row = self.ROWS[:, ts(RS["bv1"], D)]
                                return e.matmul(ps_a[:, 0:256], self.ONES1[:, 0:128],
                                                row[:, 512 * hg + 256 * vh:512 * hg + 256 * (vh + 1)],
                                                start=False, stop=True)
                            mm = p.step("T", vmm, deps=[tr_tok, wv_tok, self.tok_const,
                                                        rows_tok,
                                                        self.free.get("ps_a")])
                            ev = p.step("V", lambda e, tb=tb, vh=vh: e.tensor_copy(
                                VG[:, tb, 256 * vh:256 * (vh + 1)], ps_a[:, 0:256]),
                                deps=[mm])
                            self.free["ps_a"] = ev
                            v_done.append(ev)
                    self.free["wvb"] = v_done[-1]
                def mk_smm(qt, lh):
                    nk = (qt + 1) * 128
                    a = lh % 2
                    lft = lh // 2
                    hgl = 8 * hg + lh
                    b4 = hgl % 4

                    def smm(e):
                        last = None
                        for ck in range((nk + 511) // 512):
                            w = min(512, nk - ck * 512)
                            last = e.matmul(
                                ps_s[:, ck * 512:ck * 512 + w],
                                QTG[64 * a:64 * a + 64, lft, ts(qt)],
                                KTG[64 * a:64 * a + 64, lft, ck * 512:ck * 512 + w],
                                start=True, stop=False,
                                tile_position=(64 * a, 0), skip_group_check=True)
                            last = e.matmul(
                                ps_s[:, ck * 512:ck * 512 + w],
                                self.SCONST[32 * b4:32 * b4 + 4, hgl // 4, 0:128],
                                J4[32 * b4:32 * b4 + 4, ck * 512:ck * 512 + w],
                                start=False, stop=True,
                                tile_position=(32 * b4, 0), skip_group_check=True)
                        return last
                    return smm

                self._c_mm0 = None
                for qt in range(NT):
                    nk = (qt + 1) * 128
                    cvs = []

                    def emit_pv(pend):
                        sc, a, lh, lft, pqt, pnk = pend
                        nck = (pqt + 1 + 3) // 4
                        pes = {}
                        tks = {}

                        def mk_pvm(ckg, n4, h):
                            def pvm(e):
                                last = None
                                for i in range(n4):
                                    kt = ckg * 4 + i
                                    last = e.matmul(
                                        ps_pv[64 * a:64 * a + 64, 0:128],
                                        VG[:, kt, 64 * lh:64 * lh + 64],
                                        PTB[h][:, ts(i)],
                                        start=(kt == 0), stop=(kt == pqt),
                                        tile_position=(0, 64 * a), skip_group_check=True)
                                return last
                            return pvm

                        def emit_pvm(ckg):
                            n4 = min(4, pqt + 1 - ckg * 4)
                            h = ckg % 2
                            tok = p.step("T", mk_pvm(ckg, n4, h),
                                         deps=[pes[ckg], v_done[-1:]] +
                                         ([self.free.get(("pspv", a))]
                                          if ckg == 0 else []))
                            self.free[("ptbb", h)] = tok
                            return tok

                        pv_last = None
                        for ckg in range(nck):
                            n4 = min(4, pqt + 1 - ckg * 4)
                            h = ckg % 2
                            if ckg >= 2:
                                pv_last = emit_pvm(ckg - 2)

                            def ptt(e, ckg=ckg, n4=n4, h=h):
                                last = None
                                for i in range(n4):
                                    kt = ckg * 4 + i
                                    last = e.transpose(PSB[h][:, ts(i)],
                                                       PB[:, ts(kt)], self.IDTB[:])
                                return last
                            tks[ckg] = p.step("T", ptt,
                                              deps=[sc, self.free.get(("pstb", h))])
                            pes[ckg] = p.step("S", lambda e, ckg=ckg, n4=n4, h=h:
                                e.activation(PTB[h][:, 0:n4 * 128],
                                             PSB[h][:, 0:n4 * 128], AF.Identity),
                                deps=[tks[ckg], self.free.get(("ptbb", h))])
                            self.free[("pstb", h)] = pes[ckg]
                        for ckg in range(max(0, nck - 2), nck):
                            pv_last = emit_pvm(ckg)
                        self.free["pb_r"] = tks[nck - 1]
                        cv = p.step("S", lambda e, a=a, lft=lft: e.activation(
                            CTC[64 * a:64 * a + 64, lft, :],
                            ps_pv[64 * a:64 * a + 64, 0:128], AF.Identity),
                            deps=[pv_last, self.free.get(("ctc", lft, a))])
                        self.free[("pspv", a)] = cv
                        cvs.append(cv)

                    for lh in range(8):
                        a = lh % 2
                        lft = lh // 2
                        if lh == 0 and self._c_mm0 is not None:
                            mm = self._c_mm0
                            self._c_mm0 = None
                        else:
                            mm = p.step("T", mk_smm(qt, lh),
                                        deps=kq_done[-1:] + [self.free.get("pss"),
                                                             self.tok_const, cm_toks])
                        if self._c_pend is not None:
                            emit_pv(self._c_pend)
                            self._c_pend = None
                        t = None
                        if qt:
                            t = p.step("V", lambda e, qt=qt: e.tensor_copy(
                                SB32[:, 0:qt * 128], ps_s[:, 0:qt * 128]), deps=[mm])
                        evc = p.step("V", lambda e, qt=qt: e.tensor_tensor(
                            SB32[:, ts(qt)], ps_s[:, ts(qt)], MMASK[:], ALU.add),
                            deps=[mm] if t is None else [t])
                        self.free["pss"] = evc
                        nm = p.step("V", lambda e, nk=nk: e.tensor_reduce(
                            self.SMALL[:, 0:1], SB32[:, 0:nk], AX.X, ALU.max, negate=True),
                            deps=[evc])
                        ex = p.step("S", lambda e, nk=nk: e.activation(
                            PB[:, 0:nk], SB32[:, 0:nk], AF.Exp, bias=self.SMALL[:, 0:1],
                            accum_out=self.SMALL[:, 2:3]),
                            deps=[nm, self.free.get("pb_r")])
                        rz = p.step("V", lambda e: e.reciprocal(
                            self.SMALL[:, 3:4], self.SMALL[:, 2:3]), deps=[ex])
                        sc = p.step("V", lambda e, nk=nk: e.tensor_scalar(
                            PB[:, 0:nk], PB[:, 0:nk], self.SMALL[:, 3:4], None, ALU.mult),
                            deps=[rz])
                        self._c_pend = (sc, a, lh, lft, qt, nk)
                    if self._c_pend is not None:
                        emit_pv(self._c_pend)
                        self._c_pend = None
                    if qt + 1 < NT:
                        self._c_mm0 = p.step("T", mk_smm(qt + 1, 0),
                                             deps=kq_done[-1:] +
                                             [self.free.get("pss"),
                                              self.tok_const, cm_toks])
                    if qt == 0:
                        self._wp_tok = [p.dma(
                            "G", WPB[:],
                            dt["Wproj"][:, 4 * hg:4 * hg + 4, :],
                            deps=[self.free.get("wpb")])]

                    for hf in range(2):
                        def pjm(e, hg=hg, hf=hf):
                            last = None
                            for lfc in range(4):
                                last = e.matmul(ps_pj[:], CTC[:, lfc, :],
                                                WPB[:, lfc, ts(hf, 512)],
                                                start=(lfc == 0), stop=(lfc == 3 and hg == 1))
                            if hg == 0:
                                row = self.ROWS[:, ts(RS["bproj"], D)]
                                last = e.matmul(ps_pj[:], self.ONES1[:, 0:128],
                                                row[:, ts(hf, 512)],
                                                start=False, stop=True)
                            return last
                        mm = p.step("T", pjm, deps=(cvs if hf == 0 else []) +
                                    list(self._wp_tok) + [rows_tok,
                                                          self.free.get("ps_a")])
                        ev = p.step("V", lambda e, qt=qt, hf=hf: e.scalar_tensor_tensor(
                            self.XR[:, qt, ts(hf, 512)], ps_pj[:], 1.0,
                            self.XR[:, qt, ts(hf, 512)],
                            ALU.bypass, ALU.add), deps=[mm, prj_toks[qt]])
                        self.free["ps_a"] = ev
                        prj_toks[qt] = ev
                    for lf in range(4):
                        for aa in range(2):
                            self.free[("ctc", lf, aa)] = ev
                self.free["wpb"] = prj_toks[-1]
            x3_toks = self.layernorm(self.XR, prj_toks, 1e-5)
            return x3_toks

    def transpose_into_b(self, src, dst, ps_tb, deps, tc):
        p = self.p
        last = None
        for fc in range(NF):
            tk = p.step("T", lambda e, fc=fc, tc=tc: [
                e.transpose(ps_tb[:, ts(tt)], src[:, tc * 4 + tt, ts(fc)], self.IDTB[:])
                for tt in range(4)][-1],
                deps=list(deps) + [self.free.get("pstb"),
                                   self.free.get(("pstb", 0)),
                                   self.free.get(("pstb", 1))])
            last = p.step("V", lambda e, fc=fc: e.tensor_copy(
                dst[:, fc, :], ps_tb[:, 0:512]), deps=[tk])
            self.free["pstb"] = last
        return last


    def _quant5(self, src_ap, scale_ap, clip_hi, clip_lo, out_ap, dep, n):
        p, TMP, SCR = self.p, self.TMP, self.SCR
        i32 = TMP[:, 0, 0:n].bitcast(I32)
        t = p.step("V", lambda e: e.tensor_scalar(
            i32, src_ap, scale_ap, 256.5, ALU.mult, ALU.add), deps=[dep])
        t = p.step("V", lambda e: e.tensor_scalar(
            TMP[:, 1, 0:n], i32, 256.5, None, ALU.subtract), deps=[t])
        t = p.step("V", lambda e: e.scalar_tensor_tensor(
            SCR[:, 0:n], src_ap, scale_ap, TMP[:, 1, 0:n], ALU.mult, ALU.is_lt),
            deps=[t])
        t = p.step("V", lambda e: e.scalar_tensor_tensor(
            TMP[:, 0, 0:n], TMP[:, 1, 0:n], 0.5, SCR[:, 0:n],
            ALU.add, ALU.subtract), deps=[t])
        t = p.step("V", lambda e: e.tensor_scalar(
            out_ap, TMP[:, 0, 0:n], clip_hi, clip_lo, ALU.min, ALU.max), deps=[t])
        return t

    def _emit_w0(self):
        p, dt, nc = self.p, self.dt, self.nc
        with contextlib.ExitStack() as st:
            ps = st.enter_context(nc.psum_tensor("W0_ps", [128, 512], F32))
            ps_tb = st.enter_context(nc.psum_tensor("W0_pstb", [128, 512], BF16))
            WST = [self._sbuf("W0_WST0", (128, 1024), F32, st),
                   self._sbuf("W0_WST1", (128, 1024), F32, st)]
            QB = self._sbuf("W0_QB", (128, 1024), BF16, st)
            QTT = [self._sbuf("W0_QTT0", (128, 8, 128), BF16, st),
                   self._sbuf("W0_QTT1", (128, 8, 128), BF16, st)]
            ASUM = self._sbuf("W0_ASUM", (128, 2), F32, st)
            t0 = p.step("V", lambda e: e.memset(ASUM[:], 0.0), deps=[self.tok_const])
            ci = 0
            for wi, wname, nrc, ncol in ((0, "f_w1", 32, 1024), (1, "f_w2", 8, 4096)):
                for rc in range(nrc):
                    for qc in range(ncol // 1024):
                        w = WST[ci % 2]
                        ld = p.dma("Y", w[:], dt[wname][128 * rc:128 * (rc + 1),
                                                        1024 * qc:1024 * (qc + 1)],
                                   deps=[self.free.get(("w0wst", ci % 2))])
                        t0 = p.step("V", lambda e, w=w: e.tensor_reduce(
                            self.SCR[:, 0:1], w[:], AX.X, ALU.add,
                            apply_absolute_value=True), deps=[ld, t0])
                        t0 = p.step("V", lambda e, wi=wi: e.tensor_tensor(
                            ASUM[:, wi:wi + 1], ASUM[:, wi:wi + 1], self.SCR[:, 0:1],
                            ALU.add), deps=[t0])
                        self.free[("w0wst", ci % 2)] = t0
                        ci += 1
            mt = p.step("T", lambda e: e.matmul(ps[0:2, 0:1], ASUM[:], self.ONE128[:],
                                                start=True, stop=True), deps=[t0])
            t = p.step("V", lambda e: e.tensor_scalar(
                self.SCR[0:2, 0:1], ps[0:2, 0:1], 1.0 / (4096 * 1024), 1e-5,
                ALU.mult, ALU.max), deps=[mt])
            t = p.step("V", lambda e: e.reciprocal(self.SCR[0:2, 1:2], self.SCR[0:2, 0:1]),
                       deps=[t])
            tk1 = p.step("T", lambda e: e.transpose(
                ps[0:1, 0:2], self.SCR[0:2, 0:1], self.IDT[0:2, 0:2]), deps=[t])
            ev1 = p.step("V", lambda e: e.tensor_copy(self.SCR[0:1, 4:6], ps[0:1, 0:2]),
                         deps=[tk1])
            tk2 = p.step("T", lambda e: e.transpose(
                ps[0:1, 2:4], self.SCR[0:2, 1:2], self.IDT[0:2, 0:2]), deps=[ev1])
            ev2 = p.step("V", lambda e: e.tensor_copy(self.SCR[0:1, 6:8], ps[0:1, 2:4]),
                         deps=[tk2])
            mt2 = p.step("T", lambda e: e.matmul(
                ps[:, 0:4], self.ONES1F[:], self.SCR[0:1, 4:8], start=True, stop=True),
                deps=[ev2])
            tb_ = p.step("V", lambda e: e.tensor_copy(self.SWB[:], ps[:, 0:4]), deps=[mt2])
            self.tok_const = self.tok_const + (tb_,)
            ci = 0
            for wi, wname, nrc, ncol in ((0, "f_w1", 32, 1024), (1, "f_w2", 8, 4096)):
                for rc in range(nrc):
                    for qc in range(ncol // 1024):
                        w = WST[ci % 2]
                        qtt = QTT[ci % 2]
                        ld = p.dma("Y", w[:], dt[wname][128 * rc:128 * (rc + 1),
                                                        1024 * qc:1024 * (qc + 1)],
                                   deps=[self.free.get(("w0wst", ci % 2))])
                        qt = self._quant5(w[:], self.SWB[:, 2 + wi:3 + wi], 1.0, -1.0,
                                          QB[:], (ld, tb_), 1024)
                        self.free[("w0wst", ci % 2)] = qt
                        evs = []
                        for g in range(2):
                            tk = p.step("T", lambda e, g=g: [
                                e.transpose(ps_tb[:, ts(i)], QB[:, ts(g * 4 + i)],
                                            self.IDTB[:]) for i in range(4)][-1],
                                deps=[qt, self.free.get("w0pstb"),
                                      self.free.get(("w0qtt", ci % 2))])
                            ev = p.step("V", lambda e, g=g, qtt=qtt: e.tensor_copy(
                                qtt[:, g * 4:g * 4 + 4, :],
                                ps_tb[:].rearrange("p (c w) -> p c w", c=4)), deps=[tk])
                            self.free["w0pstb"] = ev
                            evs.append(ev)
                        if wi == 0:
                            out_ap = dt["q1t_scr"][:, :, 128 * rc:128 * (rc + 1)]
                        else:
                            out_ap = dt["q2t_scr"][:, 8 * qc:8 * qc + 8,
                                                   128 * rc:128 * (rc + 1)]
                        st_tok = p.dma("G", out_ap, qtt[:], deps=evs)
                        self.free[("w0qtt", ci % 2)] = st_tok
                        ci += 1
        p.barrier()

    def emit_phaseD(self, toks):
        self._phase_reset()
        """XR holds x3 -> x4 (BitNet FFN + module LN)."""
        p, dt, nc = self.p, self.dt, self.nc
        with contextlib.ExitStack() as st:
            def psum(name, shape, dty=F32):
                return st.enter_context(nc.psum_tensor(self._uniq(name), list(shape), dty))
            sbuf = lambda n, sh, dty: self._sbuf(n, sh, dty, st)
            ps_a = psum("D_psa", (128, 1024))
            ps_h = psum("D_psh", (128, 512))
            ps_h2 = psum("D_psh2", (128, 512))
            ps_hs = [ps_h, ps_h2]
            ps_tb = psum("D_pstb", (128, 512), BF16)

            x_dep = list(toks) if isinstance(toks, (list, tuple)) else [toks]
            DST = sbuf("D_DST", (128, 80), F32)
            MAXACC = sbuf("D_MAXACC", (128, 2048), F32)
            stA = contextlib.ExitStack()
            sbufA = lambda n, sh, dty: self._sbuf(n, sh, dty, stA)
            DEQ1BC = sbufA("D_DEQ1BC", (128, 2048), F32)
            HROWS = [sbufA("D_HROW0", (128, 2048), F32),
                     sbufA("D_HROW1", (128, 2048), F32)]
            HR2S = [sbufA("D_HR20", (128, 2048), F32),
                    sbufA("D_HR21", (128, 2048), F32)]
            XQB = sbufA("D_XQB", (128, 1024), BF16)
            xqt = self.X0B[:].rearrange("p t d -> p (t d)").rearrange(
                "p (c w) -> p c w", c=8)

            t = x_dep
            for tb in range(NT):
                tt_ = p.step("V", lambda e, tb=tb: e.tensor_reduce(
                    DST[:, tb:tb + 1], self.XR[:, tb, :], AX.X, ALU.max,
                    apply_absolute_value=True), deps=t)
                t = [tt_]
            t = p.step("V", lambda e: e.tensor_scalar(
                DST[:, 16:32], DST[:, 0:16], 1e-5, None, ALU.max), deps=t)
            t = p.step("V", lambda e: e.reciprocal(DST[:, 32:48], DST[:, 16:32]), deps=[t])
            t = p.step("V", lambda e: e.tensor_scalar(
                DST[:, 32:48], DST[:, 32:48], 127.0, None, ALU.mult), deps=[t])
            t = p.step("V", lambda e: e.reciprocal(DST[:, 48:64], DST[:, 32:48]), deps=[t])
            tdq = p.step("V", lambda e: e.tensor_scalar(
                DST[:, 48:64], DST[:, 48:64], self.SWB[:, 0:1], None, ALU.mult),
                deps=[t, self.tok_const])
            d1 = p.dma("G", dt["deq_scr"].transpose([1, 0]), DST[:, 48:64], deps=[tdq],
                       slow=True)
            d2 = p.dma("G", self.TMP[0:1, 0, :],
                       dt["deq_scr"].rearrange("t p -> (t p)").unsqueeze(0)[:, 0:1024],
                       deps=[d1])
            d3 = p.dma("G", self.TMP[0:1, 1, :],
                       dt["deq_scr"].rearrange("t p -> (t p)").unsqueeze(0)[:, 1024:2048],
                       deps=[d2])
            bc_toks = []
            for i in range(4):
                src = self.TMP[0:1, i // 2, ts(i % 2, 512)]
                mt = p.step("T", lambda e, i=i, src=src: e.matmul(
                    ps_h[:], self.ONES1F[:], src, start=True, stop=True),
                    deps=[d3, self.free.get("ps_h")])
                ev = p.step("V", lambda e, i=i: e.tensor_copy(
                    DEQ1BC[:, ts(i, 512)], ps_h[:]), deps=[mt])
                self.free["ps_h"] = ev
                bc_toks.append(ev)
            xq_last = None
            for tb in range(NT):
                qt = self._quant5(self.XR[:, tb, :], DST[:, 32 + tb:33 + tb],
                                  127.0, -128.0, XQB[:],
                                  (tdq, tuple(bc_toks), self.free.get("xqb") or bc_toks[-1]),
                                  1024)
                self.free["xqb"] = qt
                for g in range(2):
                    tk = p.step("T", lambda e, g=g: [
                        e.transpose(ps_tb[:, ts(i)], XQB[:, ts(g * 4 + i)], self.IDTB[:])
                        for i in range(4)][-1], deps=[qt, self.free.get("pstb")])
                    ev = p.step("V", lambda e, g=g, tb=tb: e.tensor_copy(
                        xqt[:, g * 4:g * 4 + 4, ts(tb)],
                        ps_tb[:].rearrange("p (c w) -> p c w", c=4)), deps=[tk])
                    self.free["pstb"] = ev
                    self.free["xqb"] = ev
                    xq_last = ev
            mx0 = p.step("V", lambda e: e.memset(MAXACC[:], 0.0), deps=x_dep)
            Q1T = [sbufA("D_Q1T0", (128, 8, 1024), BF16),
                   sbufA("D_Q1T1", (128, 8, 1024), BF16)]
            for qf in range(4):
                q1tok = p.dma("G", Q1T[qf % 2][:],
                              dt["q1t_scr"][:, :, 1024 * qf:1024 * (qf + 1)],
                              deps=[self.free.get(("q1t", qf % 2))])
                last_mm = None
                for lft in range(8):
                    ht = qf * 8 + lft
                    bi = ht % 2
                    HROW, HR2 = HROWS[bi], HR2S[bi]
                    hmm_toks = []
                    for tc in range(4):
                        ph = ps_hs[tc % 2]

                        def hmm(e, lft=lft, tc=tc, qf=qf, ph=ph):
                            last = None
                            for kc in range(NF):
                                last = e.matmul(ph[:],
                                                Q1T[qf % 2][:, kc, 128 * lft:128 * (lft + 1)],
                                                xqt[:, kc, ts(tc, 512)],
                                                start=(kc == 0), stop=(kc == NF - 1))
                            return last
                        mm = p.step("T", hmm, deps=[q1tok, xq_last,
                                                    self.free.get(("ps_h", tc % 2))])
                        last_mm = mm
                        ev = p.step("V", lambda e, tc=tc, HROW=HROW, ph=ph: e.tensor_tensor(
                            HROW[:, ts(tc, 512)], ph[:], DEQ1BC[:, ts(tc, 512)],
                            ALU.mult), deps=[mm, bc_toks[tc],
                                             self.free.get(("hrow", bi))])
                        self.free[("ps_h", tc % 2)] = ev
                        hmm_toks.append(ev)
                    cb = ht
                    t1 = p.step("V", lambda e, cb=cb, HROW=HROW, HR2=HR2: e.tensor_scalar(
                        HR2[:], HROW[:], self.CWPP[:, 1, cb:cb + 1],
                        self.BPP[:, 48 + cb:49 + cb], ALU.mult, ALU.add),
                        deps=hmm_toks + [self.free.get(("hr2", bi))])
                    t1 = p.step("V", lambda e, cb=cb, HROW=HROW, HR2=HR2: e.scalar_tensor_tensor(
                        HR2[:, 1:2048], HROW[:, 0:2047], self.CWPP[:, 0, cb:cb + 1],
                        HR2[:, 1:2048], ALU.mult, ALU.add), deps=[t1])
                    t1 = p.step("V", lambda e, cb=cb, HROW=HROW, HR2=HR2: e.scalar_tensor_tensor(
                        HR2[:, 0:2047], HROW[:, 1:2048], self.CWPP[:, 2, cb:cb + 1],
                        HR2[:, 0:2047], ALU.mult, ALU.add), deps=[t1])
                    self.free[("hrow", bi)] = t1
                    sl = p.step("S", lambda e, HROW=HROW, HR2=HR2: e.activation(
                        HROW[:], HR2[:], AF.Silu), deps=[t1])
                    ab = p.step("V", lambda e, HROW=HROW, HR2=HR2: e.scalar_tensor_tensor(
                        HR2[:], HROW[:], -1.0, HROW[:], ALU.mult, ALU.max), deps=[sl, mx0])
                    mx0 = p.step("V", lambda e, HR2=HR2: e.tensor_tensor(
                        MAXACC[:], HR2[:], MAXACC[:], ALU.max), deps=[ab])
                    self.free[("hr2", bi)] = mx0
                    sp = p.dma("G", dt["h2_scr"][ht], HROW[:], deps=[mx0])
                    self.free[("hrow", bi)] = sp
                self.free[("q1t", qf % 2)] = last_mm
            s2t = []
            for tg in range(4):
                tk = p.step("T", lambda e, tg=tg: [
                    e.transpose(ps_h[:, ts(i)], MAXACC[:, ts(tg * 4 + i)],
                                self.IDT[:]) for i in range(4)][-1],
                    deps=[mx0, self.free.get("ps_h")])
                rd = p.step("V", lambda e, tg=tg: e.tensor_reduce(
                    DST[:, 64 + tg * 4:64 + tg * 4 + 4],
                    ps_h[:].rearrange("p (c w) -> p c w", c=4), AX.X, ALU.max),
                    deps=[tk])
                self.free["ps_h"] = rd
                s2t.append(rd)
            t = p.step("V", lambda e: e.tensor_scalar(
                DST[:, 0:16], DST[:, 64:80], 1e-5, None, ALU.max), deps=s2t)
            t = p.step("V", lambda e: e.reciprocal(DST[:, 16:32], DST[:, 0:16]), deps=[t])
            ts2 = p.step("V", lambda e: e.tensor_scalar(
                DST[:, 16:32], DST[:, 16:32], 127.0, None, ALU.mult), deps=[t])
            t = p.step("V", lambda e: e.reciprocal(DST[:, 32:48], DST[:, 16:32]), deps=[ts2])
            tdq2 = p.step("V", lambda e: e.tensor_scalar(
                DST[:, 32:48], DST[:, 32:48], self.SWB[:, 1:2], None, ALU.mult),
                deps=[t])
            d1 = p.dma("G", dt["deq_scr"].transpose([1, 0]), DST[:, 16:32],
                       deps=[ts2, tdq2], slow=True)
            d2 = p.dma("G", self.TMP[0:1, 0, :],
                       dt["deq_scr"].rearrange("t p -> (t p)").unsqueeze(0)[:, 0:1024],
                       deps=[d1])
            d3 = p.dma("G", self.TMP[0:1, 1, :],
                       dt["deq_scr"].rearrange("t p -> (t p)").unsqueeze(0)[:, 1024:2048],
                       deps=[d2])
            S2BC = MAXACC
            s2bc_toks = []
            for i in range(4):
                src = self.TMP[0:1, i // 2, ts(i % 2, 512)]
                mt = p.step("T", lambda e, i=i, src=src: e.matmul(
                    ps_h[:], self.ONES1F[:], src, start=True, stop=True),
                    deps=[d3, self.free.get("ps_h")])
                ev = p.step("V", lambda e, i=i: e.tensor_copy(
                    S2BC[:, ts(i, 512)], ps_h[:]), deps=[mt, mx0])
                self.free["ps_h"] = ev
                s2bc_toks.append(ev)
            stA.close()
            p.barrier()
            Q2T = sbuf("D_Q2T", (128, 8, 1024), BF16)
            Q2C = sbuf("D_Q2C", (128, 8, 512), BF16)
            H2T = [sbuf("D_H2T0", (128, 512), F32), sbuf("D_H2T1", (128, 512), F32)]
            acc_toks = [None] * NT
            for qf in range(4):
                q2tok = p.dma("G", Q2T[:], dt["q2t_scr"][:, 8 * qf:8 * (qf + 1), :],
                              deps=[self.free.get("q2t")])
                for tc in range(4):
                    qc_toks = []
                    for lhc in range(8):
                        ht = qf * 8 + lhc
                        hl = p.dma("Y", H2T[lhc % 2][:], dt["h2_scr"][ht][:, ts(tc, 512)],
                                   deps=[self.free.get(("h2t", lhc % 2))])
                        qt = self._quant6(H2T[lhc % 2][:], S2BC[:, ts(tc, 512)], Q2C,
                                          lhc, (hl, tuple(s2bc_toks),
                                                self.free.get("q2c") or s2bc_toks[-1]), 512)
                        self.free[("h2t", lhc % 2)] = qt
                        qc_toks.append(qt)
                    for tt in range(4):
                        tb = tc * 4 + tt

                        def ymm(e, tt=tt, qf=qf):
                            last = None
                            for hf in range(2):
                                for lhc in range(8):
                                    last = e.matmul(
                                        ps_a[:, ts(hf, 512)], Q2C[:, lhc, ts(tt)],
                                        Q2T[:, lhc, ts(hf, 512)],
                                        start=(lhc == 0), stop=(lhc == 7))
                            return last
                        mm = p.step("T", ymm, deps=qc_toks + [q2tok,
                                                              self.free.get("ps_a")])
                        if tt == 3:
                            self.free["q2c"] = mm
                            if tc == 3:
                                self.free["q2t"] = mm
                        ev = p.step("V", lambda e, tb=tb: e.scalar_tensor_tensor(
                            self.XR[:, tb, :], ps_a[:], DST[:, 32 + tb:33 + tb],
                            self.XR[:, tb, :], ALU.mult, ALU.add),
                            deps=[mm, acc_toks[tb] or tdq2])
                        self.free["ps_a"] = ev
                        acc_toks[tb] = ev
            x4_toks = self.layernorm(self.XR, acc_toks, 1e-5)
            return x4_toks

    def _quant6(self, src_ap, scale_row_ap, q2c, lhc, dep, n):
        p, TMP = self.p, self.TMP
        t = p.step("V", lambda e: e.tensor_tensor(
            TMP[:, 0, 0:n], src_ap, scale_row_ap, ALU.mult), deps=[dep])
        i32 = TMP[:, 1, 0:n].bitcast(I32)
        t = p.step("V", lambda e: e.tensor_scalar(
            i32, TMP[:, 0, 0:n], 256.5, None, ALU.add), deps=[t])
        t = p.step("V", lambda e: e.tensor_scalar(
            TMP[:, 0, 512:512 + n], i32, 256.5, None, ALU.subtract), deps=[t])
        t = p.step("V", lambda e: e.tensor_tensor(
            TMP[:, 1, 0:n], TMP[:, 0, 0:n], TMP[:, 0, 512:512 + n], ALU.is_lt), deps=[t])
        t = p.step("V", lambda e: e.scalar_tensor_tensor(
            TMP[:, 0, 0:n], TMP[:, 0, 512:512 + n], 0.5, TMP[:, 1, 0:n],
            ALU.add, ALU.subtract), deps=[t])
        t = p.step("V", lambda e, lhc=lhc: e.tensor_scalar(
            q2c[:, lhc, :], TMP[:, 0, 0:n], 127.0, -128.0, ALU.min, ALU.max), deps=[t])
        return t

    def _emit(self):
        p, dt = self.p, self.dt
        self.load_consts()
        for b in range(self.batches):
            self._cur_batch = b
            x_tok = p.dma("Y", self.XR[:].rearrange("p t d -> p (t d)"), dt["x"][b],
                          deps=[self.free.get("xr_store")])
            toks = [x_tok] * NT
            for ph, dump in [("A", "x1"), ("B", "x2"), ("C", "x3"), ("D", None)]:
                if ph in self.phases:
                    toks = getattr(self, "emit_phase" + ph)(toks)
                    p.barrier()
                    toks = [("V", self.p.cnt["V"])] * NT
                if dump and dump in self.dumps:
                    dtok = p.dma("G", dt["dump_" + dump][b],
                                 self.XR[:].rearrange("p t d -> p (t d)"), deps=toks)
                    self.store_toks.append(dtok)
                    p._wait("G", dtok)
            stok = p.dma("G", dt["out"][b], self.XR[:].rearrange("p t d -> p (t d)"),
                         deps=toks)
            self.store_toks.append(stok)
            self.free["xr_store"] = stok
        p.waits("G", self.store_toks)

    def build(self):
        with self.stack:
            with self.nc.Block() as block:
                self.p.finalize(block)
        return self.nc



def prepare_shared_inputs(ins):
    g = {}
    C = np.ascontiguousarray
    def pcf(W):
        K, F = W.shape
        return C(W.reshape(K // 128, 128, F).transpose(1, 0, 2))
    g["Wq"] = pcf(ins["L0_attn_Wq"])
    g["Wk"] = pcf(ins["L0_attn_Wk"])
    g["Wv"] = pcf(ins["L0_attn_Wv"])
    g["Wo"] = pcf(ins["L0_attn_Wo"])
    W1 = ins["L0_attn_W1"]
    g["W1f"] = C(W1.reshape(8, 128, 32, 128).transpose(2, 1, 0, 3))
    g["W2"] = pcf(ins["L0_attn_W2"])
    for n, k in [("bq", "L0_attn_bq"), ("bk", "L0_attn_bk"), ("bv", "L0_attn_bv"),
                 ("bo", "L0_attn_bo"), ("b2", "L0_attn_b2")]:
        g[n + "_row"] = C(ins[k].reshape(1, D))
    g["b1_pp"] = C(ins["L0_attn_b1"].reshape(32, 128).T)
    for n, k in [("g1", "L0_attn_g1"), ("be1", "L0_attn_be1"),
                 ("g2", "L0_attn_g2"), ("be2", "L0_attn_be2")]:
        g[n + "_bc"] = C(np.broadcast_to(ins[k], (128, D)))
    g["choice_wT"] = pcf(ins["L0_ffn_choice_w"].T)
    w1 = ins["L0_ffn_w1"]
    g["moe_w1f"] = C(w1.reshape(8, 8, 128, 16, 128).transpose(0, 2, 3, 1, 4))
    g["moe_w2"] = C(ins["L0_ffn_w2"].reshape(8, 16, 128, D).transpose(0, 2, 1, 3))
    Wkqv = ins["L1_attn_W_kqv"]
    g["Wkqv_f"] = C(Wkqv[:, :2048].reshape(8, 128, 16, 128).transpose(2, 1, 0, 3))
    g["Wv1"] = pcf(Wkqv[:, 2048:])
    g["Wproj"] = pcf(ins["L1_attn_W_proj"])
    g["bv1_row"] = C(ins["L1_attn_b_kqv"][2048:].reshape(1, D))
    g["bproj_row"] = C(ins["L1_attn_b_proj"].reshape(1, D))
    g["slopes_row"] = C(ins["L1_attn_slopes"].reshape(1, 16))
    g["f_w1"] = C(ins["L1_ffn_w1"])
    g["f_w2"] = C(ins["L1_ffn_w2"])
    g["f_convw_pp"] = C(ins["L1_ffn_conv_w"][:, 0, :].reshape(32, 128, 3).transpose(1, 2, 0))
    g["f_convb_pp"] = C(ins["L1_ffn_conv_b"].reshape(32, 128).T)
    bk1 = ins["L1_attn_b_kqv"][:D]
    bq1 = ins["L1_attn_b_kqv"][D:2 * D]
    g["bk1_pp"] = C(bk1.reshape(8, 128).T)
    g["bq1_pp"] = C(bq1.reshape(8, 128).T)
    g.update(host_constants())
    return g


def shard_x(x, c):
    xc = x[BPC * c:BPC * (c + 1)]
    return np.ascontiguousarray(
        xc.reshape(BPC, NT, 128, D).transpose(0, 2, 1, 3).reshape(BPC, 128, NT * D))


def unshard_out(res_list):
    outs = []
    for r in res_list:
        o = r["out"].reshape(BPC, 128, NT, D).transpose(0, 2, 1, 3).reshape(BPC, S, D)
        outs.append(o)
    return np.concatenate(outs, 0)


def kernel(**inputs):
    core = Core(phases=("A", "B", "C", "D"))
    nc = core.build()
    shared = prepare_shared_inputs(inputs)
    in_maps = []
    for c in range(NCORES):
        m = dict(shared)
        m["x"] = shard_x(np.asarray(inputs["L0_attn_x"]), c)
        in_maps.append(m)
    res = run_bass_kernel_spmd(nc, in_maps, core_ids=list(range(NCORES)))
    return unshard_out(res.results).astype(np.float32)


# version 16
# speedup vs baseline: 1.0069x; 1.0069x over previous
import contextlib
import numpy as np
import ml_dtypes

import concourse.bass as bass
import concourse.mybir as mybir
from concourse.bass_utils import run_bass_kernel_spmd

F32 = mybir.dt.float32
BF16 = mybir.dt.bfloat16
I32 = mybir.dt.int32
AF = mybir.ActivationFunctionType
ALU = mybir.AluOpType
AX = mybir.AxisListType
BF = ml_dtypes.bfloat16

NCORES = 8
BPC = 2
S = 2048
D = 1024
NT = S // 128
NF = D // 128

ENG_ATTR = {"T": "tensor", "V": "vector", "S": "scalar", "G": "gpsimd", "Y": "sync"}


class Prog:
    def __init__(self, nc, stack):
        self.nc = nc
        self.q = {e: [] for e in ENG_ATTR}
        self.sem = {}
        for name in ["T", "V", "S", "G", "Y", "YD", "GD", "SD"]:
            self.sem[name] = stack.enter_context(nc.semaphore("sem" + name))
        self.cnt = {k: 0 for k in self.sem}
        self.waited = {e: {} for e in ENG_ATTR}

    @staticmethod
    def _flat(deps):
        for d in deps:
            if d is None:
                continue
            if isinstance(d, tuple) and len(d) == 2 and isinstance(d[0], str):
                yield d
            else:
                yield from Prog._flat(d)

    def _wait(self, eng, tok):
        if tok is None:
            return
        sem_name, val = tok
        if self.waited[eng].get(sem_name, 0) >= val:
            return
        self.waited[eng][sem_name] = val
        sem = self.sem[sem_name]
        self.q[eng].append(lambda e, sem=sem, val=val: e.wait_ge(sem, val))

    def step(self, eng, fn, deps=()):
        if eng in ("V", "S", "G") and self.cnt[eng]:
            self._wait(eng, (eng, self.cnt[eng]))
        for d in self._flat(deps):
            self._wait(eng, d)
        self.cnt[eng] += 1
        sem = self.sem[eng]

        def thunk(e, fn=fn, sem=sem, eng=eng):
            inst = fn(e)
            assert inst is not None, "step fn must return its last instruction"
            inst.then_inc(sem, 1)
        self.q[eng].append(thunk)
        return (eng, self.cnt[eng])

    def dma(self, eng, out, in_, deps=(), slow=False):
        sem_name = {"Y": "YD", "G": "GD", "S": "SD"}[eng]
        if self.cnt[sem_name]:
            self._wait(eng, (sem_name, self.cnt[sem_name]))
        for d in self._flat(deps):
            self._wait(eng, d)
        self.cnt[sem_name] += 16
        sem = self.sem[sem_name]
        self.q[eng].append(
            lambda e, out=out, in_=in_, sem=sem, slow=slow: e.dma_start(
                out=out, in_=in_, allow_slow_non_contiguous=slow).then_inc(sem, 16))
        return (sem_name, self.cnt[sem_name])

    def waits(self, eng, deps):
        for d in self._flat(deps):
            self._wait(eng, d)

    def dma_fn(self, eng, fn, deps=(), chain=True):
        sem_name = {"Y": "YD", "G": "GD", "S": "SD"}[eng]
        if chain and self.cnt[sem_name]:
            self._wait(eng, (sem_name, self.cnt[sem_name]))
        for d in self._flat(deps):
            self._wait(eng, d)
        self.cnt[sem_name] += 16
        sem = self.sem[sem_name]
        self.q[eng].append(
            lambda e, fn=fn, sem=sem: fn(e).then_inc(sem, 16))
        return (sem_name, self.cnt[sem_name])

    def barrier(self, engs="TVSGY"):
        toks = [(e, self.cnt[e]) for e in engs if self.cnt[e]] +                [(q, self.cnt[q]) for q in ("YD", "GD", "SD") if self.cnt[q]]
        for e in engs:
            for t in toks:
                self._wait(e, t)

    def finalize(self, block):
        for eng, attr in ENG_ATTR.items():
            thunks = self.q[eng]

            def section(e, thunks=thunks):
                for t in thunks:
                    t(e)
            getattr(block, attr)(section)


def ts(i, n=128):
    return slice(i * n, (i + 1) * n)


def host_constants():
    c = {}
    c["identity_f32"] = np.ascontiguousarray(np.eye(128, dtype=np.float32))
    c["ones_1x128_f32"] = np.ones((1, 128), np.float32)
    c["blockones2"] = np.zeros((128, 2), np.float32)
    c["blockones2"][:64, 0] = 1.0
    c["blockones2"][64:, 1] = 1.0
    c["sel2"] = np.zeros((2, 128), np.float32)
    c["sel2"][0, :64] = 1.0
    c["sel2"][1, 64:] = 1.0
    j = np.arange(S, dtype=np.float32)
    j_hi = np.floor(j / 128) * 128
    j_lo = j - j_hi
    j4 = np.stack([j_hi, j_lo, j_hi, j_lo])
    j4pack = np.zeros((128, S), np.float32)
    for a in range(4):
        j4pack[32 * a:32 * a + 4] = j4
    c["j4pack"] = j4pack
    c["i_local"] = np.arange(128, dtype=np.float32).reshape(1, 128)
    c["qtramp"] = ((np.arange(256, dtype=np.float32) % 16) * 128.0).reshape(1, 256)
    m = np.zeros((128, 128), np.float32)
    ii, jj = np.mgrid[0:128, 0:128]
    m[jj > ii] = -1e30
    c["mmask"] = m
    c["pcol"] = np.arange(128, dtype=np.float32).reshape(128, 1)
    return c


class Core:
    def __init__(self, phases=("A", "B", "C", "D"), dumps=(), batches=BPC):
        self.phases = phases
        self.dumps = dumps
        self.batches = batches
        self.nc = bass.Bass("TRN2", target_bir_lowering=False)
        self.stack = contextlib.ExitStack()
        self.p = Prog(self.nc, self.stack)
        self.dt = {}
        self.sb = {}
        self.free = {}
        self.store_toks = []
        self._declare_dram()
        self._alloc_persistent()
        self._emit()

    def _din(self, name, shape, dtype=F32):
        self.dt[name] = self.nc.dram_tensor(name, list(shape), dtype, kind="ExternalInput").ap()

    def _dout(self, name, shape, dtype=F32):
        self.dt[name] = self.nc.dram_tensor(name, list(shape), dtype, kind="ExternalOutput").ap()

    def _declare_dram(self):
        din = self._din
        din("x", (self.batches, 128, NT * D))
        for n in ["Wq", "Wk", "Wv", "Wo"]:
            din(n, (128, 8, D))
        din("W1f", (32, 128, 8, 128))
        din("W2", (128, 32, D))
        for n in ["bq", "bk", "bv", "bo", "b2", "bv1", "bproj"]:
            din(n + "_row", (1, D))
        din("b1_pp", (128, 32))
        for n in ["g1", "be1", "g2", "be2"]:
            din(n + "_bc", (128, D))
        din("choice_wT", (128, 8, 8))
        din("moe_w1f", (8, 128, 16, 8, 128))
        din("moe_w2", (8, 128, 16, D))
        din("Wkqv_f", (16, 128, 8, 128))
        din("Wv1", (128, 8, D))
        din("Wproj", (128, 8, D))
        din("bk1_pp", (128, 8))
        din("bq1_pp", (128, 8))
        din("slopes_row", (1, 16))
        din("f_w1", (4096, D))
        din("f_w2", (D, 4096))
        din("f_convw_pp", (128, 3, 32))
        din("f_convb_pp", (128, 32))
        for k, v in host_constants().items():
            din(k, v.shape)
        self.dt["q1t_scr"] = self.nc.dram_tensor("q1t_scr", [128, 8, 4096], BF16).ap()
        self.dt["q2t_scr"] = self.nc.dram_tensor("q2t_scr", [128, 32, 1024], BF16).ap()
        self.dt["h2_scr"] = self.nc.dram_tensor("h2_scr", [32, 128, 2048], F32).ap()
        self.dt["deq_scr"] = self.nc.dram_tensor("deq_scr", [16, 128], F32).ap()
        self.dt["thr_scr"] = self.nc.dram_tensor("thr_scr", [1, 8], F32).ap()

        self.dt["xs_scr"] = self.nc.dram_tensor("xs_scr", [8, 512, 1024], BF16).ap()
        self.dt["y_scr"] = self.nc.dram_tensor("y_scr", [2048, 1024], F32).ap()
        self.dt["gid_scr"] = self.nc.dram_tensor("gid_scr", [8, 512, 2], F32).ap()
        self.dt["moe_w1b"] = self.nc.dram_tensor(
            "moe_w1b", [8, 128, 16, 8, 128], BF16).ap()
        self.dt["moe_w2b"] = self.nc.dram_tensor(
            "moe_w2b", [8, 128, 16, 1024], BF16).ap()
        self._dout("out", (self.batches, 128, NT * D))
        for d in self.dumps:
            self._dout("dump_" + d, (self.batches, 128, NT * D))

    def _uniq(self, name):
        self._uid = getattr(self, "_uid", 0) + 1
        return f"{name}_u{self._uid}"

    def _sbuf(self, name, shape, dtype, stack=None):
        t = (stack or self.stack).enter_context(
            self.nc.sbuf_tensor(self._uniq(name), list(shape), dtype))
        self.sb[name] = t
        return t

    def _alloc_persistent(self):
        sbuf = self._sbuf
        self.XR = sbuf("XR", (128, NT, D), F32)
        self.X0B = sbuf("X0B", (128, NT, D), BF16)
        self.IDT = sbuf("IDT", (128, 128), F32)
        self.ONES1 = sbuf("ONES1", (1, 128), BF16)
        self.ONES1F = sbuf("ONES1F", (1, 128), F32)
        self.ONESR = sbuf("ONESR", (1, 512), BF16)
        self.BO2 = sbuf("BO2", (128, 2), F32)
        self.SEL2 = sbuf("SEL2", (2, 128), BF16)
        self.ROWS = sbuf("ROWS", (1, 5 * D), BF16)
        self.BPP = sbuf("BPP", (128, 128), F32)
        self.SMALL = sbuf("SMALL", (128, 64), F32)
        self.TMP = sbuf("TMP", (128, 2, 1024), F32)
        self.SCR = sbuf("SCR", (128, 1024), F32)
        self.IDTB = sbuf("IDTB", (128, 128), BF16)
        self.SCONST = sbuf("SCONST", (128, 4, 128), BF16)
        self.SLP = sbuf("SLP", (1, 128), F32)
        self.SLB = sbuf("SLB", (1, 128), BF16)
        self.SWB = sbuf("SWB", (128, 4), F32)
        self.CWPP = sbuf("CWPP", (128, 3, 32), F32)
        self.ONE128 = sbuf("ONE128", (128, 1), F32)

    def _phase_reset(self):
        self.free = {k: v for k, v in self.free.items() if k in ("xr_store",)}

    def dump_xr(self, name, deps):
        if "dump_" + name not in self.dt:
            return
        tok = self.p.dma("G", self.dt["dump_" + name][self._cur_batch],
                         self.XR[:].rearrange("p t d -> p (t d)"), deps=deps)
        self.p._wait("G", tok)
        self.store_toks.append(tok)

    def load_consts(self):
        p, dt = self.p, self.dt
        ld = []
        ld.append(p.dma("Y", self.IDT[:], dt["identity_f32"]))
        ld.append(p.dma("Y", self.ONES1F[:], dt["ones_1x128_f32"]))
        ld.append(p.dma("G", self.ONES1[:], dt["ones_1x128_f32"]))
        ld.append(p.dma("Y", self.BO2[:], dt["blockones2"]))
        ld.append(p.dma("G", self.SEL2[:], dt["sel2"]))
        self.ROW_SLOT = {}
        ld.append(p.dma("Y", self.BPP[:, 0:32], dt["b1_pp"]))
        ld.append(p.dma("Y", self.BPP[:, 32:40], dt["bk1_pp"]))
        ld.append(p.dma("Y", self.BPP[:, 40:48], dt["bq1_pp"]))
        ld.append(p.dma("Y", self.BPP[:, 48:80], dt["f_convb_pp"]))
        ld.append(p.dma("G", self.IDTB[:], dt["identity_f32"]))
        ld.append(p.dma("Y", self.SLP[:, 0:16], dt["slopes_row"]))
        ld.append(p.dma("Y", self.CWPP[:], dt["f_convw_pp"]))
        t = p.step("V", lambda e: e.memset(self.ONESR[:], 1.0))
        t2 = p.step("V", lambda e: e.memset(self.ONE128[:], 1.0))
        self.tok_const = tuple(ld) + (t, t2)
        self._gen_attn_consts()
        self._emit_w0()

    def _emit_moe_cast(self, CH):
        p, dt, nc = self.p, self.dt, self.nc
        if True:
            ci = 0
            for e_i in range(8):
                for c in range(4):
                    for src, dst in (
                        (dt["moe_w1f"][e_i][:, 4 * c:4 * c + 4],
                         dt["moe_w1b"][e_i][:, 4 * c:4 * c + 4]),
                        (dt["moe_w2"][e_i][:, 4 * c:4 * c + 4, :],
                         dt["moe_w2b"][e_i][:, 4 * c:4 * c + 4, :]),
                    ):
                        b = CH[ci % 2]
                        v = b[:].rearrange("p (a w) -> p a w", a=4)
                        ld = p.dma("G", v, src,
                                   deps=[self.free.get(("mcch", ci % 2))])
                        stk = p.dma("Y", dst, v, deps=[ld])
                        self.free[("mcch", ci % 2)] = stk
                        ci += 1
            self._mc_tok = stk

    def _gen_attn_consts(self):
        p, nc = self.p, self.nc
        with contextlib.ExitStack() as st:
            ps = st.enter_context(nc.psum_tensor("CG_ps", [128, 256], F32))
            cd = [self.tok_const]
            t = p.step("V", lambda e: e.tensor_copy(self.SLB[:, 0:16], self.SLP[:, 0:16]),
                       deps=cd)
            t = p.step("V", lambda e: e.tensor_copy(self.SLP[:, 16:32], self.SLB[:, 0:16]),
                       deps=[t])
            t = p.step("V", lambda e: e.tensor_tensor(
                self.SLP[:, 32:48], self.SLP[:, 0:16], self.SLP[:, 16:32], ALU.subtract),
                deps=[t])
            t = p.step("V", lambda e: e.tensor_copy(self.SLB[:, 16:32], self.SLP[:, 32:48]),
                       deps=[t])
            sh4 = self.SLB[:, 64:128].rearrange("o (h r) -> o h r", r=4)
            for j, src in [(0, 16), (1, 16), (2, 16 + 16), (3, 16 + 16)]:
                srcsl = self.SLB[:, 0:16] if j < 2 else self.SLB[:, 16:32]
                t = p.step("V", lambda e, j=j, srcsl=srcsl: e.tensor_copy(
                    sh4[:, :, j:j + 1], srcsl.unsqueeze(2)), deps=[t])
            gt = t
            for h in range(16):
                b = h % 4
                mt = p.step("T", lambda e, h=h, b=b: e.matmul(
                    ps[32 * b:32 * b + 4, 0:128], self.SLB[:, 64 + 4 * h:64 + 4 * h + 4],
                    self.ONES1[:, 0:128], start=True, stop=True,
                    tile_position=(0, 32 * b), skip_group_check=True), deps=[gt])
                gt = p.step("V", lambda e, h=h, b=b: e.tensor_copy(
                    self.SCONST[32 * b:32 * b + 4, h // 4, :], ps[32 * b:32 * b + 4, 0:128]),
                    deps=[mt])
            self.tok_const = self.tok_const + (gt,)
        p.barrier()

    def load_rows(self, names):
        self.ROW_SLOT = {}
        toks = []
        for i, n in enumerate(names):
            self.ROW_SLOT[n] = i
            toks.append(self.p.dma("G", self.ROWS[:, ts(i, D)], self.dt[n + "_row"]))
        self.rows_tok = tuple(toks)
        return self.rows_tok

    def rank1(self, e, psum_ap, slot, n=D, stop=True):
        row = self.ROWS[:, ts(slot, D)]
        last = None
        for h in range(0, n, 512):
            w = min(512, n - h)
            last = e.matmul(psum_ap[:, h:h + w], self.ONES1[:, 0:128],
                            row[:, h:h + w], start=False, stop=stop)
        return last

    def rank1_fm(self, e, psum_ap, slot, ftile, ntok, stop=True):
        row = self.ROWS[:, ts(slot, D)][:, ts(ftile)]
        return e.matmul(psum_ap, row, self.ONESR[:, 0:ntok], start=False, stop=stop)

    def transpose_into(self, src_ap_fn, dst, dstcol, ps, deps, bank="ps_t", src_tb_base=0):
        p = self.p
        last_ev = None
        for fc in range(NF):
            tk = p.step("T", lambda e, fc=fc: [
                e.transpose(ps[:, ts(tt)], src_ap_fn(src_tb_base + tt, fc), self.IDT[:])
                for tt in range(4)][-1],
                deps=list(deps) + [self.free.get(bank)])
            ev = p.step("V", lambda e, fc=fc: e.tensor_copy(
                dst[:, fc, dstcol:dstcol + 512], ps[:]), deps=[tk])
            self.free[bank] = ev
            last_ev = ev
        return last_ev

    def layernorm(self, src, dep_toks, eps, gamma=None, beta=None, dst=None,
                  add_src=None, extra_deps=()):
        p = self.p
        if dst is None:
            dst = src
        SM = self.SMALL
        stat_toks = []
        for tb in range(NT):
            d = [dep_toks[tb]] if isinstance(dep_toks, (list, tuple)) else [dep_toks]
            t1 = p.step("V", lambda e, tb=tb: e.scalar_tensor_tensor(
                self.SCR[:], src[:, tb, :], 1.0, src[:, tb, :], ALU.bypass, ALU.mult,
                accum_out=SM[:, 16 + tb:17 + tb]), deps=d)
            t2 = p.step("V", lambda e, tb=tb: e.reduce_sum(
                SM[:, tb:tb + 1], src[:, tb, :], AX.X), deps=d)
            stat_toks += [t1, t2]

        tf = p.step("V", lambda e: e.tensor_scalar(
            SM[:, 32:48], SM[:, 0:16], -1.0 / D, None, ALU.mult), deps=stat_toks)
        tf = p.step("V", lambda e: e.tensor_tensor(
            self.SCR[:, 0:16], SM[:, 32:48], SM[:, 32:48], ALU.mult), deps=[tf])
        tf = p.step("V", lambda e: e.scalar_tensor_tensor(
            self.SCR[:, 16:32], SM[:, 16:32], 1.0 / D, self.SCR[:, 0:16],
            ALU.mult, ALU.subtract), deps=[tf])
        tf = p.step("V", lambda e: e.tensor_scalar(
            self.SCR[:, 16:32], self.SCR[:, 16:32], eps, None, ALU.add), deps=[tf])
        tf = p.step("V", lambda e: e.reciprocal(
            self.SCR[:, 32:48], self.SCR[:, 16:32]), deps=[tf])
        tr = p.step("S", lambda e: e.activation(SM[:, 48:64], self.SCR[:, 32:48], AF.Sqrt),
                    deps=[tf])
        out_toks = []
        for tb in range(NT):
            simple = gamma is None and add_src is None
            t = p.step("V", lambda e, tb=tb, simple=simple: e.tensor_scalar(
                dst[:, tb, :] if simple else self.TMP[:, 0, :],
                src[:, tb, :], SM[:, 32 + tb:33 + tb], SM[:, 48 + tb:49 + tb],
                ALU.add, ALU.mult), deps=[tr] + list(extra_deps))
            if gamma is not None:
                t = p.step("V", lambda e: e.tensor_tensor(
                    self.TMP[:, 1, :], self.TMP[:, 0, :], gamma, ALU.mult), deps=[t])
                if add_src is None:
                    t = p.step("V", lambda e, tb=tb: e.tensor_tensor(
                        dst[:, tb, :], self.TMP[:, 1, :], beta, ALU.add), deps=[t])
                else:
                    t = p.step("V", lambda e, tb=tb: e.tensor_tensor(
                        self.TMP[:, 0, :], self.TMP[:, 1, :], beta, ALU.add), deps=[t])
                    t = p.step("V", lambda e, tb=tb: e.tensor_tensor(
                        dst[:, tb, :], self.TMP[:, 0, :], add_src[:, tb, :], ALU.add),
                        deps=[t])
            elif add_src is not None:
                t = p.step("V", lambda e, tb=tb: e.tensor_tensor(
                    dst[:, tb, :], self.TMP[:, 0, :], add_src[:, tb, :], ALU.add), deps=[t])
            out_toks.append(t)
        return out_toks

    def emit_phaseA(self, x_toks):
        self._phase_reset()
        p, dt, nc = self.p, self.dt, self.nc
        with contextlib.ExitStack() as st:
            def psum(name, w):
                return st.enter_context(nc.psum_tensor(self._uniq(name), [128, w], F32))
            sbuf = lambda n, sh, dty: self._sbuf(n, sh, dty, st)

            ps_t = psum("A_pst", 512)
            ps_q = [psum("A_psq0", 512), psum("A_psq1", 512)]
            ps_o = [psum("A_pso0", 1024), psum("A_pso1", 1024)]
            ps_kv = psum("A_pskv", 512)
            ps_ss = ps_t

            TMPB = sbuf("A_TMPB", (128, 2, 1024), F32)
            TMP, SCR = self.TMP, self.SCR
            SROW = sbuf("SROW", (2, 512), F32)
            SROWB = sbuf("SROWB", (2, 512), BF16)
            x_dep = list(x_toks) if isinstance(x_toks, (list, tuple)) else [x_toks]

            x0b_toks = [p.step("V", lambda e, tb=tb: e.tensor_copy(
                self.X0B[:, tb, :], self.XR[:, tb, :]), deps=x_dep) for tb in range(NT)]
            rows_tok = self.load_rows(["bq", "bk", "bv", "bo", "b2"])
            RS = dict(self.ROW_SLOT)

            XTC = [sbuf("A_XTC0", (128, 8, 512), BF16), sbuf("A_XTC1", (128, 8, 512), BF16)]
            st2 = contextlib.ExitStack()
            sbuf2 = lambda n, sh, dty: self._sbuf(n, sh, dty, st2)
            WB = [sbuf2("A_WB0", (128, 8, D), BF16), sbuf2("A_WB1", (128, 8, D), BF16)]
            QNR = sbuf2("A_QNR", (128, 2, 512), BF16)
            KNR = sbuf2("A_KNR", (128, 2, D), BF16)
            VBR = sbuf2("A_VBR", (128, 2, D), BF16)
            CTXC = [sbuf2("A_CTXC0", (128, 8, 512), BF16)]

            ctok = (self.tok_const, rows_tok)
            wk_tok = p.dma("G", WB[0][:], dt["Wk"])
            wv_tok = p.dma("G", WB[1][:], dt["Wv"])
            kv_last = None
            for tc in range(4):
                cur = XTC[tc % 2]
                tr_tok = self.transpose_into(
                    lambda tb, fc: self.XR[:, tb, ts(fc)], cur, 0, ps_t, x_dep,
                    src_tb_base=tc * 4)
                for pr in range(2):
                    ek_toks = {}
                    for u in range(2):
                        tt = pr * 2 + u
                        tb = tc * 4 + tt
                        sl = tb % 2
                        pk = ps_o[sl]

                        def kmm(e, tt=tt, pk=pk, cur=cur):
                            for hf in range(2):
                                for kc in range(NF):
                                    e.matmul(pk[:, ts(hf, 512)], cur[:, kc, ts(tt)],
                                             WB[0][:, kc, ts(hf, 512)],
                                             start=(kc == 0), stop=False)
                            return self.rank1(e, pk[:], RS["bk"])
                        mm = p.step("T", kmm, deps=[tr_tok, wk_tok, ctok,
                                                    self.free.get(("pso", sl))])
                        t2 = p.step("S", lambda e, pk=pk, sl=sl: e.activation(
                            TMPB[:, sl, :], pk[:], AF.Exp), deps=[mm])
                        m1 = p.step("V", lambda e, sl=sl: e.tensor_scalar(
                            TMP[:, sl, :], TMPB[:, sl, :], 1.0, -1.0, ALU.min, ALU.add),
                            deps=[t2])
                        ek = p.step("V", lambda e, pk=pk, sl=sl: e.scalar_tensor_tensor(
                            TMP[:, sl, :], pk[:], 0.0, TMP[:, sl, :], ALU.max, ALU.add),
                            deps=[m1])
                        self.free[("pso", sl)] = ek
                        ek_toks[sl] = ek
                        sq = p.step("V", lambda e, sl=sl: e.tensor_tensor(
                            SCR[:].rearrange("p (h d) -> p h d", d=64),
                            TMP[:, sl, :].rearrange("p (h d) -> p h d", d=64),
                            TMP[:, sl, :].rearrange("p (h d) -> p h d", d=64),
                            ALU.mult), deps=[ek])
                        rd = p.step("V", lambda e, sl=sl: e.tensor_reduce(
                            self.SMALL[:, 16 * sl:16 * sl + 16],
                            SCR[:].rearrange("p (h d) -> p h d", d=64),
                            AX.X, ALU.add), deps=[sq])
                    rcp = p.step("V", lambda e: e.reciprocal(
                        self.SMALL[:, 32:64], self.SMALL[:, 0:32]), deps=[rd])
                    rsq = p.step("S", lambda e: e.activation(
                        self.SMALL[:, 0:32], self.SMALL[:, 32:64], AF.Sqrt), deps=[rcp])
                    pend = []
                    for u in range(2):
                        tt = pr * 2 + u
                        tb = tc * 4 + tt
                        sl = tb % 2
                        kn_t = p.step("V", lambda e, sl=sl: e.tensor_tensor(
                            KNR[:, sl, :].rearrange("p (h d) -> p h d", d=64),
                            TMP[:, sl, :].rearrange("p (h d) -> p h d", d=64),
                            self.SMALL[:, 16 * sl:16 * sl + 16].unsqueeze(2)
                            .broadcast_to([128, 16, 64]),
                            ALU.mult), deps=[rsq, ek_toks[sl],
                                             self.free.get(("knr", sl))])
                        pv = ps_o[(tb + 1) % 2]

                        def vmm(e, tt=tt, pv=pv, cur=cur):
                            for hf in range(2):
                                for kc in range(NF):
                                    e.matmul(pv[:, ts(hf, 512)], cur[:, kc, ts(tt)],
                                             WB[1][:, kc, ts(hf, 512)],
                                             start=(kc == 0), stop=False)
                            return self.rank1(e, pv[:], RS["bv"])
                        mmv = p.step("T", vmm, deps=[wv_tok, ctok,
                                                     self.free.get(("pso", (tb + 1) % 2))])
                        vev = p.step("V", lambda e, pv=pv, sl=sl: e.tensor_copy(
                            VBR[:, sl, :], pv[:]), deps=[mmv, self.free.get(("vbr", sl))])
                        self.free[("pso", (tb + 1) % 2)] = vev
                        pend.append((tb, sl, kn_t, vev))
                    for tb, sl, kn_t, vev in pend:
                        def kvmm(e, tb=tb, sl=sl):
                            last = None
                            for h in range(16):
                                a = h % 2
                                last = e.matmul(
                                    ps_kv[ts(a, 64), (h // 2) * 64:(h // 2) * 64 + 64],
                                    KNR[:, sl, ts(h, 64)], VBR[:, sl, ts(h, 64)],
                                    start=(tb == 0 and h <= 1),
                                    stop=(tb == NT - 1 and h >= 14),
                                    tile_position=(0, 64 * a), skip_group_check=True)
                            return last
                        kv_last = p.step("T", kvmm, deps=[kn_t, vev])
                        self.free[("vbr", sl)] = kv_last
                        self.free[("knr", sl)] = kv_last
            KVS = KNR
            kv_ev = p.step("V", lambda e: e.tensor_scalar(
                KVS[:, 0, 0:512], ps_kv[:], 0.125, None, ALU.mult), deps=[kv_last])
            wq_tok = p.dma("G", WB[0][:], dt["Wq"], deps=[kv_last])
            wo_tok = p.dma("G", WB[1][:], dt["Wo"], deps=[kv_last])
            s1_toks = [None] * NT
            for tc in range(4):
                cur = XTC[tc % 2]
                ctxc = CTXC[0]
                tr_tok = self.transpose_into(
                    lambda tb, fc: self.XR[:, tb, ts(fc)], cur, 0, ps_t,
                    x_dep + [kv_last], src_tb_base=tc * 4)
                ctx_done = []
                for fp in range(4):
                    eq_toks = {}
                    for u in range(2):
                        ft = fp * 2 + u
                        sl = ft % 2
                        pq = ps_q[sl]

                        def qmm(e, ft=ft, pq=pq, cur=cur):
                            for kc in range(NF):
                                e.matmul(pq[:], WB[0][:, kc, ts(ft)], cur[:, kc, :],
                                         start=(kc == 0), stop=False)
                            return self.rank1_fm(e, pq[:], RS["bq"], ft, 512)
                        mm = p.step("T", qmm, deps=[tr_tok, wq_tok, ctok,
                                                    self.free.get(("psq", sl))])
                        t2 = p.step("S", lambda e, pq=pq, sl=sl: e.activation(
                            TMPB[:, sl, 0:512], pq[:], AF.Exp), deps=[mm])
                        m1 = p.step("V", lambda e, sl=sl: e.tensor_scalar(
                            TMP[:, sl, 0:512], TMPB[:, sl, 0:512], 1.0, -1.0,
                            ALU.min, ALU.add), deps=[t2])
                        eq = p.step("V", lambda e, pq=pq, sl=sl: e.scalar_tensor_tensor(
                            TMP[:, sl, 512:1024], pq[:], 0.0, TMP[:, sl, 0:512],
                            ALU.max, ALU.add), deps=[m1])
                        self.free[("psq", sl)] = eq
                        eq_toks[sl] = eq
                        sq = p.step("V", lambda e, sl=sl: e.tensor_tensor(
                            TMP[:, sl, 0:512], TMP[:, sl, 512:1024], TMP[:, sl, 512:1024],
                            ALU.mult), deps=[eq])
                    for u in range(2):
                        ft = fp * 2 + u
                        sl = ft % 2
                        pq = ps_q[sl]
                        ssmm = p.step("T", lambda e, sl=sl: e.matmul(
                            ps_ss[0:2, :], self.BO2[:], TMP[:, sl, 0:512],
                            start=True, stop=True),
                            deps=[sq, self.free.get("ps_t")])
                        rc = p.step("V", lambda e, sl=sl: e.reciprocal(
                            SROW[0:2, 0:512], ps_ss[0:2, :]),
                            deps=[ssmm, self.free.get("srow")])
                        rq = p.step("S", lambda e, sl=sl: e.activation(
                            SROWB[0:2, 0:512], SROW[0:2, 0:512], AF.Sqrt),
                            deps=[rc, self.free.get("srowb")])
                        self.free["srow"] = rq
                        bc = p.step("T", lambda e, sl=sl: e.matmul(
                            ps_ss[:, :], self.SEL2[:], SROWB[0:2, 0:512],
                            start=True, stop=True),
                            deps=[rq])
                        self.free["srowb"] = bc
                        qn_t = p.step("V", lambda e, sl=sl: e.tensor_tensor(
                            QNR[:, sl, :], TMP[:, sl, 512:1024], ps_ss[:, :], ALU.mult),
                            deps=[bc, eq_toks[sl], self.free.get(("qnr", sl))])
                        self.free["ps_t"] = qn_t

                        def cmm(e, ft=ft, pq=pq, sl=sl):
                            last = None
                            for a in range(2):
                                h = 2 * ft + a
                                last = e.matmul(
                                    pq[ts(a, 64), :],
                                    KVS[ts(a, 64), 0, (h // 2) * 64:(h // 2) * 64 + 64],
                                    QNR[ts(a, 64), sl, :],
                                    start=True, stop=True,
                                    tile_position=(64 * a, 64 * a), skip_group_check=True)
                            return last
                        mm2 = p.step("T", cmm, deps=[kv_ev, qn_t])
                        ev = p.step("V", lambda e, ft=ft, pq=pq, ctxc=ctxc: e.tensor_copy(
                            ctxc[:, ft, :], pq[:]), deps=[mm2, self.free.get(("ctxc", ft))])
                        self.free[("psq", sl)] = ev
                        self.free[("qnr", sl)] = mm2
                        ctx_done.append(ev)
                for tt in range(4):
                    tb = tc * 4 + tt
                    po = ps_o[tt % 2]

                    def omm(e, tt=tt, po=po, ctxc=ctxc):
                        for hf in range(2):
                            for kc in range(NF):
                                e.matmul(po[:, ts(hf, 512)], ctxc[:, kc, ts(tt)],
                                         WB[1][:, kc, ts(hf, 512)],
                                         start=(kc == 0), stop=False)
                        return self.rank1(e, po[:], RS["bo"])
                    mm = p.step("T", omm, deps=ctx_done + [wo_tok, ctok,
                                                           self.free.get(("pso", tt % 2))])
                    ev = p.step("V", lambda e, tb=tb, po=po: e.scalar_tensor_tensor(
                        self.XR[:, tb, :], po[:], 1.0, self.XR[:, tb, :],
                        ALU.bypass, ALU.add), deps=[mm, x0b_toks[tb]])
                    self.free[("pso", tt % 2)] = ev
                    s1_toks[tb] = ev
                for ft in range(NF):
                    self.free[("ctxc", ft)] = s1_toks[tc * 4 + 3]
            self.dump_xr("a_s1", s1_toks)
            st2.close()
            BCT = sbuf("A_BCT", (128, 4, D), F32)
            bct_toks = [p.dma("Y", BCT[:, i, :], dt[n + "_bc"], deps=s1_toks)
                        for n, i in [("g1", 0), ("be1", 1), ("g2", 2), ("be2", 3)]]
            a_toks = self.layernorm(self.XR, s1_toks, 1e-12,
                                    gamma=BCT[:, 0, :], beta=BCT[:, 1, :],
                                    extra_deps=bct_toks)
            self.dump_xr("a_attn", a_toks)
            ATC = XTC
            HC4 = sbuf("A_HC4", (128, 4, 512), BF16)
            W1G = [sbuf("A_W1G0", (128, 4, 8, 128), BF16),
                   sbuf("A_W1G1", (128, 4, 8, 128), BF16)]
            W2S = [sbuf("A_W2S0", (128, 4, D), BF16), sbuf("A_W2S1", (128, 4, D), BF16)]
            s2_toks = [None] * NT
            for tc in range(4):
                atc = ATC[tc % 2]
                at_tok = self.transpose_into(
                    lambda tb, fc: self.XR[:, tb, ts(fc)], atc, 0, ps_t,
                    a_toks[tc * 4:tc * 4 + 4] + [kv_last], src_tb_base=tc * 4)
                for g in range(8):
                    w2tok = p.dma("G", W2S[g % 2][:],
                                  dt["W2"][:, g * 4:(g + 1) * 4, :],
                                  deps=[self.free.get(("w2s", g % 2), s1_toks[-1])])
                    w1tok = p.dma("G", W1G[g % 2][:],
                                  dt["W1f"][g * 4:(g + 1) * 4].rearrange("g p c f -> p g c f"),
                                  deps=[self.free.get(("w1g", g % 2), s1_toks[-1])])
                    h_done = []
                    for fi in range(4):
                        ft = g * 4 + fi
                        pq = ps_q[fi % 2]

                        def hmm(e, fi=fi, pq=pq, atc=atc, g=g):
                            last = None
                            for kc in range(NF):
                                last = e.matmul(pq[:], W1G[g % 2][:, fi, kc, :],
                                                atc[:, kc, :],
                                                start=(kc == 0), stop=(kc == NF - 1))
                            return last
                        mm = p.step("T", hmm, deps=[at_tok, w1tok,
                                                    self.free.get(("psq", fi % 2))])
                        hv = p.step("S", lambda e, fi=fi, pq=pq, ft=ft: e.activation(
                            HC4[:, fi, :], pq[:], AF.Gelu, bias=self.BPP[:, ft:ft + 1]),
                            deps=[mm, self.tok_const, self.free.get(("hc4", fi))])
                        self.free[("psq", fi % 2)] = hv
                        h_done.append(hv)
                    self.free[("w1g", g % 2)] = h_done[-1]
                    for tt in range(4):
                        tb = tc * 4 + tt
                        po = ps_o[tt % 2]

                        def w2mm(e, g=g, tt=tt, po=po):
                            last = None
                            for hf in range(2):
                                for fi in range(4):
                                    last = e.matmul(po[:, ts(hf, 512)], HC4[:, fi, ts(tt)],
                                                    W2S[g % 2][:, fi, ts(hf, 512)],
                                                    start=(fi == 0), stop=(fi == 3 and g != 0))
                            if g == 0:
                                return self.rank1(e, po[:], RS["b2"])
                            return last
                        mm = p.step("T", w2mm, deps=h_done + [w2tok,
                                                              self.free.get(("pso", tt % 2))])
                        ev = p.step("V", lambda e, tb=tb, po=po: e.tensor_tensor(
                            self.XR[:, tb, :], po[:], self.XR[:, tb, :], ALU.add),
                            deps=[mm, s2_toks[tb] or a_toks[tb]])
                        self.free[("pso", tt % 2)] = ev
                        s2_toks[tb] = ev
                    for fi in range(4):
                        self.free[("hc4", fi)] = s2_toks[tc * 4 + 3]
                    self.free[("w2s", g % 2)] = s2_toks[tc * 4 + 3]
            self.dump_xr("a_s2", s2_toks)
            s3_toks = self.layernorm(self.XR, s2_toks, 1e-12,
                                     gamma=BCT[:, 2, :], beta=BCT[:, 3, :],
                                     add_src=self.X0B, extra_deps=bct_toks)
            self.dump_xr("a_s3", s3_toks)
            x1_toks = self.layernorm(self.XR, s3_toks, 1e-5)
            return x1_toks

    def emit_phaseB(self, toks):
        self._phase_reset()
        """XR holds x1 -> x2 (expert-choice MoE + module LN)."""
        p, dt, nc = self.p, self.dt, self.nc
        with contextlib.ExitStack() as st:
            def psum(name, shape):
                return st.enter_context(nc.psum_tensor(self._uniq(name), list(shape), F32))
            sbuf = lambda n, sh, dty: self._sbuf(n, sh, dty, st)

            ps_t = psum("B_pst", (128, 512))
            ps_q = [psum("B_psq0", (128, 512)), psum("B_psq1", (128, 512))]
            ps_o = [psum("B_pso0", (128, 1024)), psum("B_pso1", (128, 1024))]
            ps_tbB = st.enter_context(nc.psum_tensor(
                self._uniq("B_pstb"), [128, 512], BF16))
            ps_r = ps_q[0][:, 0:8]

            TMP, SCR = self.TMP, self.SCR
            x_dep = list(toks) if isinstance(toks, (list, tuple)) else [toks]
            GIDALL = sbuf("B_GIDALL", (128, 16, 8, 2), F32)
            POSI32 = sbuf("B_POSI", (128, 16, 8), I32)

            with contextlib.ExitStack() as st2:
                sbuf2 = lambda n, sh, dty: self._sbuf(n, sh, dty, st2)
                if self._cur_batch == 0:
                    CH = [sbuf2("MC_CH0", (128, 4096), BF16),
                          sbuf2("MC_CH1", (128, 4096), BF16)]
                    self._emit_moe_cast(CH)
                X1TF = sbuf2("B_X1TF", (128, 8, 512), F32)
                CW = sbuf2("B_CW", (128, 8, 8), F32)
                cw_tok = p.dma("Y", CW[:], dt["choice_wT"])
                PR = sbuf2("B_PR", (128, 16, 8), F32)
                GT = sbuf2("B_GT", (128, 16, 8), F32)
                PT8 = sbuf2("B_PT8", (8, 2048), F32)
                BMS = sbuf2("B_BMS", (8, 2048), BF16)
                BV = sbuf2("B_BV", (8, 8), F32)
                THRB = sbuf2("B_THRB", (128, 8), F32)
                CSF = sbuf2("B_CSF", (8, 2048), F32)
                POST = sbuf2("B_POST", (128, 16, 8), F32)
                ZB = sbuf2("B_ZB", (128, 1024), F32)
                xcp = [p.step("V", lambda e, tb=tb: e.tensor_copy(
                    self.X0B[:, tb, :], self.XR[:, tb, :]), deps=x_dep)
                    for tb in range(NT)]
                pr_toks = []
                for tc in range(4):
                    ev32 = None
                    for fc in range(NF):
                        tk = p.step("T", lambda e, fc=fc, tc=tc: [
                            e.transpose(ps_t[:, ts(tt)], self.XR[:, tc * 4 + tt, ts(fc)],
                                        self.IDT[:]) for tt in range(4)][-1],
                            deps=x_dep + [self.free.get("ps_t")])
                        ev32 = p.step("V", lambda e, fc=fc: e.tensor_copy(
                            X1TF[:, fc, :], ps_t[:]), deps=[tk, self.free.get(("x1tf", fc))])
                        self.free["ps_t"] = ev32
                    for tt in range(4):
                        tb = tc * 4 + tt

                        def rmm(e, tt=tt):
                            last = None
                            for kc in range(NF):
                                last = e.matmul(ps_r, X1TF[:, kc, ts(tt)], CW[:, kc, :],
                                                start=(kc == 0), stop=(kc == NF - 1))
                            return last
                        mm = p.step("T", rmm, deps=[ev32, cw_tok, self.free.get("psr")])
                        nm = p.step("V", lambda e: e.tensor_reduce(
                            self.SMALL[:, 0:1], ps_r, AX.X, ALU.max, negate=True),
                            deps=[mm])
                        ex = p.step("S", lambda e, tb=tb: e.activation(
                            PR[:, tb, :], ps_r, AF.Exp, bias=self.SMALL[:, 0:1],
                            accum_out=self.SMALL[:, 1:2]), deps=[nm])
                        rz = p.step("V", lambda e: e.reciprocal(
                            self.SMALL[:, 2:3], self.SMALL[:, 1:2]), deps=[ex])
                        pn = p.step("V", lambda e, tb=tb: e.tensor_scalar(
                            PR[:, tb, :], PR[:, tb, :], self.SMALL[:, 2:3], None, ALU.mult),
                            deps=[rz])
                        self.free["psr"] = pn
                        pr_toks.append(pn)
                    for fc in range(NF):
                        self.free[("x1tf", fc)] = pr_toks[-1]
                pt_toks = []
                for tg in range(4):
                    tk = p.step("T", lambda e, tg=tg: [
                        e.transpose(ps_t[0:8, ts(tt)], PR[:, tg * 4 + tt, :], self.IDT[:])
                        for tt in range(4)][-1],
                        deps=pr_toks[tg * 4:tg * 4 + 4] + [self.free.get("ps_t")])
                    ev = p.step("V", lambda e, tg=tg: e.tensor_copy(
                        PT8[0:8, ts(tg, 512)], ps_t[0:8, :]), deps=[tk])
                    self.free["ps_t"] = ev
                    pt_toks.append(ev)
                t = p.step("V", lambda e: e.memset(BV[:, 0:1], 0.0), deps=pt_toks)
                t = p.step("V", lambda e: e.memset(BV[:, 1:2], 1.0), deps=[t])
                for it in range(36):
                    t = p.step("V", lambda e: e.tensor_tensor(
                        BV[:, 2:3], BV[:, 0:1], BV[:, 1:2], ALU.add), deps=[t])
                    t = p.step("V", lambda e: e.tensor_scalar(
                        BV[:, 2:3], BV[:, 2:3], 0.5, None, ALU.mult), deps=[t])
                    t = p.step("V", lambda e: e.tensor_scalar(
                        BMS[0:8, :], PT8[0:8, :], BV[0:8, 2:3], 0.0, ALU.is_gt, ALU.add,
                        accum_out=BV[:, 3:4]), deps=[t])
                    t = p.step("V", lambda e: e.tensor_scalar(
                        BV[:, 4:5], BV[:, 3:4], 512.0, None, ALU.is_ge), deps=[t])
                    t = p.step("V", lambda e: e.scalar_tensor_tensor(
                        BV[:, 5:6], BV[:, 2:3], 1.0, BV[:, 0:1], ALU.bypass, ALU.subtract),
                        deps=[t])
                    t = p.step("V", lambda e: e.tensor_tensor(
                        BV[:, 5:6], BV[:, 5:6], BV[:, 4:5], ALU.mult), deps=[t])
                    t = p.step("V", lambda e: e.tensor_tensor(
                        BV[:, 0:1], BV[:, 0:1], BV[:, 5:6], ALU.add), deps=[t])
                    t = p.step("V", lambda e: e.scalar_tensor_tensor(
                        BV[:, 5:6], BV[:, 1:2], 1.0, BV[:, 2:3], ALU.bypass, ALU.subtract),
                        deps=[t])
                    t = p.step("V", lambda e: e.tensor_tensor(
                        BV[:, 5:6], BV[:, 5:6], BV[:, 4:5], ALU.mult), deps=[t])
                    t = p.step("V", lambda e: e.scalar_tensor_tensor(
                        BV[:, 1:2], BV[:, 5:6], 1.0, BV[:, 2:3], ALU.bypass, ALU.add),
                        deps=[t])
                bd1 = p.dma("G", dt["thr_scr"].transpose([1, 0]), BV[0:8, 0:1],
                            deps=[t], slow=True)
                bt = p.dma("G", THRB[:], dt["thr_scr"].broadcast_to([128, 8]), deps=[bd1])
                gt_toks = []
                for tb in range(NT):
                    g1 = p.step("V", lambda e, tb=tb: e.tensor_tensor(
                        SCR[:, 0:8], PR[:, tb, :], THRB[:], ALU.is_gt), deps=[bt])
                    g2 = p.step("V", lambda e, tb=tb: e.tensor_tensor(
                        GT[:, tb, :], PR[:, tb, :], SCR[:, 0:8], ALU.mult), deps=[g1])
                    gt_toks.append(g2)
                POSF = sbuf2("B_POSF", (128, 16, 8), F32)
                PCF = sbuf2("B_PCF", (128, 1), F32)
                zb0 = p.step("V", lambda e: e.memset(ZB[:], 0.0), deps=x_dep)
                z_toks = [p.dma("Y", dt["y_scr"].rearrange(
                    "(t p) d -> p t d", p=128)[:, tb, :], ZB[:], deps=[zb0])
                    for tb in range(NT)]
                pc_tok = p.dma("Y", PCF[:], dt["pcol"])
                gidi = p.step("V", lambda e: e.tensor_copy(
                    GIDALL[:, :, :, 0], GT[:]), deps=gt_toks)
                gidt = None
                for tb in range(NT):
                    gidt = p.step("V", lambda e, tb=tb: e.tensor_scalar(
                        GIDALL[:, tb, :, 1], PCF[:, 0:1].broadcast_to([128, 8]),
                        float(tb * 128), None, ALU.add),
                        deps=[pc_tok, gidi])
                mk = p.step("V", lambda e: e.tensor_scalar(
                    CSF[0:8, :], PT8[0:8, :], BV[0:8, 0:1], None, ALU.is_gt),
                    deps=[bd1, pt_toks])
                t = p.step("V", lambda e: e.tensor_tensor_scan(
                    PT8[0:8, 0:1024], CSF[0:8, 0:1024],
                    CSF[0:8, 0:1024], 0.0, ALU.add, ALU.bypass), deps=[mk])
                t = p.step("V", lambda e: e.tensor_tensor_scan(
                    PT8[0:8, 1024:2048], CSF[0:8, 1024:2048],
                    CSF[0:8, 1024:2048], 0.0, ALU.add, ALU.bypass), deps=[t])
                t = p.step("V", lambda e: e.scalar_tensor_tensor(
                    PT8[0:8, 1024:2048],
                    PT8[0:8, 1023:1024].broadcast_to([8, 1024]), 1.0,
                    PT8[0:8, 1024:2048], ALU.bypass, ALU.add), deps=[t])
                t = p.step("V", lambda e: e.tensor_scalar(
                    PT8[0:8, :], PT8[0:8, :], 999999.0, None, ALU.add), deps=[t])
                pos2 = p.step("V", lambda e: e.scalar_tensor_tensor(
                    CSF[0:8, :], CSF[0:8, :], -1000000.0, PT8[0:8, :],
                    ALU.mult, ALU.add), deps=[t])
                ptt_toks = []
                for tb in range(NT):
                    tk = p.step("T", lambda e, tb=tb: e.transpose(
                        ps_t[:, 0:8], CSF[0:8, ts(tb)], self.IDT[0:8, 0:8]),
                        deps=[pos2, self.free.get("ps_t")])
                    ev = p.step("V", lambda e, tb=tb: e.tensor_copy(
                        POSF[:, tb, :], ps_t[:, 0:8]), deps=[tk])
                    self.free["ps_t"] = ev
                    ptt_toks.append(ev)
                pi = p.step("V", lambda e: e.tensor_copy(POSI32[:], POSF[:]),
                            deps=ptt_toks)
                BREG = self.__dict__.setdefault("_breg", {})

                def mkreg(e):
                    if "r" not in BREG:
                        BREG["r"] = e.alloc_register("bnd511")
                    return e.reg_mov(BREG["r"], 511)
                rtok = p.step("G", mkreg, deps=[])
                IOA = bass.IndirectOffsetOnAxis
                xs_flat = dt["xs_scr"].rearrange("e s d -> (e s) d")
                gid_flat = dt["gid_scr"].rearrange("e s d -> (e s) d")
            W1H = sbuf("B_W1H", (128, 8, 8, 128), BF16)
            W2H = sbuf("B_W2H", (128, 16, 512), BF16)
            HE = sbuf("B_HE", (128, 16, 512), BF16)
            XSEL = sbuf("B_XSEL", (128, 4, 1024), BF16)
            XSELF = sbuf("B_XSELF", (128, 8, 512), BF16)
            YS = sbuf("B_YS", (128, 8, 512), F32)
            GIDL = sbuf("B_GIDL", (128, 4, 2), F32)
            ID32 = sbuf("B_ID32", (128, 4), I32)
            ya_last = None
            for e_i in range(8):
                sct = None
                for tb in range(NT):
                    sct = p.dma_fn("G", lambda e, e_i=e_i, tb=tb:
                        e.indirect_dma_start(
                            xs_flat, IOA(POSI32[:, tb, e_i:e_i + 1], 0),
                            self.X0B[:, tb, :], None, bounds_check=BREG["r"],
                            oob_is_err=False,
                            element_offset=e_i * 512 * 1024),
                        deps=[xcp[tb], pi, rtok], chain=(tb == 0))
                git = None
                for tb in range(NT):
                    git = p.dma_fn("G", lambda e, e_i=e_i, tb=tb:
                        e.indirect_dma_start(
                            gid_flat, IOA(POSI32[:, tb, e_i:e_i + 1], 0),
                            GIDALL[:, tb, e_i, :], None, bounds_check=BREG["r"],
                            oob_is_err=False,
                            element_offset=e_i * 512 * 2),
                        deps=[gidt, pi, rtok], chain=(tb == 0))
                xl = p.dma("Y", XSEL[:], dt["xs_scr"][e_i].rearrange(
                    "(c p) d -> p c d", p=128),
                    deps=[git, self.free.get("xsel")])
                gl = p.dma("Y", GIDL[:], dt["gid_scr"][e_i].rearrange(
                    "(c p) d -> p c d", p=128),
                    deps=[git, self.free.get("gidl")])
                idt = p.step("V", lambda e: e.tensor_copy(
                    ID32[:], GIDL[:, :, 1]), deps=[gl, self.free.get("id32")])
                tr_toks = []
                for kc in range(NF):
                    tk = p.step("T", lambda e, kc=kc: [
                        e.transpose(ps_tbB[:, ts(t)], XSEL[:, t, ts(kc)],
                                    self.IDTB[:]) for t in range(4)][-1],
                        deps=[xl, self.free.get("pstbB")])
                    evt = p.step("V", lambda e, kc=kc: e.tensor_copy(
                        XSELF[:, kc, :], ps_tbB[:]), deps=[tk])
                    self.free["pstbB"] = evt
                    tr_toks.append(evt)
                self.free["xsel"] = tr_toks[-1]
                h_done = []
                for hh in range(2):
                    w1tok = p.dma("Y", W1H[:],
                                  dt["moe_w1b"][e_i][:, 8 * hh:8 * hh + 8],
                                  deps=[self.free.get("w1h")])
                    for lf in range(8):
                        lft = hh * 8 + lf
                        pq = ps_q[lft % 2]

                        def hmm(e, lf=lf, pq=pq):
                            last = None
                            for kc in range(NF):
                                last = e.matmul(pq[:], W1H[:, lf, kc, :],
                                                XSELF[:, kc, :],
                                                start=(kc == 0), stop=(kc == NF - 1))
                            return last
                        mm = p.step("T", hmm, deps=[w1tok, tr_toks[-1:],
                                                    self.free.get(("psq", lft % 2))])
                        if lf == 7:
                            self.free["w1h"] = mm
                        hv = p.step("S", lambda e, lft=lft, pq=pq: e.activation(
                            HE[:, lft, :], pq[:], AF.Silu),
                            deps=[mm, self.free.get(("he", lft))])
                        self.free[("psq", lft % 2)] = hv
                        h_done.append(hv)
                for hf in range(2):
                    w2tok = p.dma("Y", W2H[:],
                                  dt["moe_w2b"][e_i][:, :, 512 * hf:512 * hf + 512],
                                  deps=[self.free.get("w2h")])
                    for tt in range(4):
                        po = ps_o[tt % 2]

                        def ymm(e, tt=tt, po=po, hf=hf):
                            last = None
                            for fc in range(16):
                                last = e.matmul(po[:, ts(hf, 512)], HE[:, fc, ts(tt)],
                                                W2H[:, fc, :],
                                                start=(fc == 0), stop=(fc == 15))
                            return last
                        mm = p.step("T", ymm, deps=h_done + [w2tok,
                                                             self.free.get(("pso", tt % 2, hf))])
                        if tt == 3:
                            self.free["w2h"] = mm
                            if hf == 1:
                                for lft in range(16):
                                    self.free[("he", lft)] = mm
                        par = tt * 2 + hf
                        ev = p.step("V", lambda e, tt=tt, po=po, hf=hf, par=par:
                            e.tensor_scalar(
                                YS[:, par, :], po[:, ts(hf, 512)], GIDL[:, tt, 0:1],
                                None, ALU.mult),
                            deps=[mm, gl, self.free.get("ya_grp")])
                        self.free[("pso", tt % 2, hf)] = ev
                        ya = p.dma_fn("G", lambda e, tt=tt, hf=hf, par=par:
                            e.indirect_dma_start(
                                dt["y_scr"], IOA(ID32[:, tt:tt + 1], 0), YS[:, par, :],
                                None, compute_op=ALU.add,
                                element_offset=512 * hf),
                            deps=[ev, idt] + ([z_toks] if e_i == 0 else []),
                            chain=(tt == 0 and hf == 0))
                        ya_last = ya
                self.free["ya_grp"] = ya_last
                self.free["id32"] = ya_last
                self.free["gidl"] = ya_last
            acc_toks = []
            for tb in range(NT):
                par = (tb % 4) * 2
                yl = p.dma("Y", YS[:, par:par + 2, :].rearrange("p a w -> p (a w)"),
                           dt["y_scr"].rearrange(
                    "(t p) d -> p t d", p=128)[:, tb, :],
                    deps=[ya_last, self.free.get(("yl", par))])
                av = p.step("V", lambda e, tb=tb, par=par:
                    e.scalar_tensor_tensor(
                        self.XR[:, tb, :],
                        YS[:, par:par + 2, :].rearrange("p a w -> p (a w)"), 1.0,
                        self.XR[:, tb, :],
                        ALU.bypass, ALU.add), deps=[yl])
                self.free[("yl", par)] = av
                acc_toks.append(av)
            x2_toks = self.layernorm(self.XR, acc_toks, 1e-5)
            return x2_toks

    def emit_phaseC(self, toks):
        self._phase_reset()
        """XR holds x2 -> x3 (causal ALiBi attention + module LN)."""
        p, dt, nc = self.p, self.dt, self.nc
        with contextlib.ExitStack() as st:
            def psum(name, shape, dty=F32):
                return st.enter_context(nc.psum_tensor(self._uniq(name), list(shape), dty))
            sbuf = lambda n, sh, dty: self._sbuf(n, sh, dty, st)

            ps_s = psum("C_pss", (128, 2048))
            ps_a = psum("C_psa", (128, 512))
            ps_t = ps_a[:]
            ps_tb = psum("C_pstb", (128, 512), BF16)
            ps_tb2 = psum("C_pstb2", (128, 512), BF16)
            PSB = [ps_tb, ps_tb2]
            ps_pv = psum("C_pspv", (128, 512))
            ps_pj = ps_a

            x_dep = list(toks) if isinstance(toks, (list, tuple)) else [toks]
            self._c_pend = None
            xtv = self.X0B[:].rearrange("p t d -> p (t d)").rearrange(
                "p (c w) -> p c w", c=8)
            xt_toks = [self.transpose_into(
                lambda tb, fc: self.XR[:, tb, ts(fc)], xtv, tc * 512, ps_a,
                x_dep, bank="ps_a", src_tb_base=tc * 4) for tc in range(4)]
            x0b_toks = xt_toks
            rows_tok = self.load_rows(["bv1", "bproj"])
            RS = dict(self.ROW_SLOT)
            bqs = p.step("V", lambda e: e.tensor_scalar(
                self.BPP[:, 80:88], self.BPP[:, 40:48], 0.125, None, ALU.mult),
                deps=[self.tok_const])

            MMASK = sbuf("C_MMASK", (128, 128), F32)
            SB32 = sbuf("C_SB32", (128, 2048), F32)
            J4 = sbuf("C_J4", (128, S), BF16)
            cm_toks = (p.dma("Y", MMASK[:], dt["mmask"]),
                       p.dma("G", J4[:], dt["j4pack"]))
            KTG = sbuf("C_KTG", (128, 4, S), BF16)
            QTG = sbuf("C_QTG", (128, 4, S), BF16)
            VG = sbuf("C_VG", (128, NT, 512), BF16)
            WPB = sbuf("C_WPB", (128, 4, 1024), BF16)
            _xtc = sbuf("C_XTC0", (128, 8, 512), BF16)
            XTC = [_xtc, _xtc]
            _xf = _xtc[:].rearrange("p c w -> p (c w)")
            PB = _xf[:, 0:2048]
            PTB = [_xf[:, 2048:2560], _xf[:, 3072:3584]]
            CTC = _xf[:, 2560:3072].rearrange("p (c w) -> p c w", c=4)
            _wkb = sbuf("C_WKB0", (128, 8, 128), BF16)
            WKB = [_wkb, _wkb]
            WVB = sbuf("C_WVB", (128, 8, 256), BF16)

            prj_toks = [xt_toks[-1]] * NT
            for hg in range(2):
                def load_wv(vh):
                    return p.dma("G", WVB[:], dt["Wv1"]
                        [:, :, 512 * hg + 256 * vh:512 * hg + 256 * (vh + 1)],
                        deps=[self.free.get("wvb")])
                kq_done = []
                for tc in range(4):
                    tr_tok = xt_toks[tc]
                    cur = xtv[:, :, ts(tc, 512)].rearrange("p c w -> p c w")
                    for src, dstT, bias_col, blk0 in (
                            ("k", KTG, 32, 4 * hg), ("q", QTG, 56, 8 + 4 * hg)):
                        for lft in range(4):
                            wtok = p.dma("G", WKB[lft % 2][:],
                                         dt["Wkqv_f"][blk0 + lft],
                                         deps=[self.free.get("wkb")])
                            def pmm(e, lft=lft, cur=cur):
                                last = None
                                for kc in range(NF):
                                    last = e.matmul(ps_t, WKB[lft % 2][:, kc, :],
                                                    cur[:, kc, :],
                                                    start=(kc == 0), stop=(kc == NF - 1))
                                return last
                            mm = p.step("T", pmm, deps=[tr_tok, wtok,
                                                        self.free.get("ps_a")])
                            if src == "k":
                                ev = p.step("S", lambda e, lft=lft, tc=tc, dstT=dstT, bias_col=bias_col, hg=hg: e.activation(
                                    dstT[:, lft, ts(tc, 512)], ps_t, AF.Identity,
                                    bias=self.BPP[:, bias_col + 4 * hg + lft:bias_col + 4 * hg + lft + 1]),
                                    deps=[mm])
                            else:
                                ev = p.step("S", lambda e, lft=lft, tc=tc, dstT=dstT, hg=hg: e.activation(
                                    dstT[:, lft, ts(tc, 512)], ps_t, AF.Identity,
                                    bias=self.BPP[:, 80 + 4 * hg + lft:81 + 4 * hg + lft],
                                    scale=0.125),
                                    deps=[mm, bqs])
                            self.free["wkb"] = ev
                            self.free["ps_a"] = ev
                            kq_done.append(ev)
                v_done = []
                for vh in range(2):
                    wv_tok = load_wv(vh)
                    for tc in range(4):
                        tr_tok = xt_toks[tc]
                        cur = xtv[:, :, ts(tc, 512)].rearrange("p c w -> p c w")
                        for tt in range(4):
                            tb = tc * 4 + tt

                            def vmm(e, tt=tt, cur=cur, vh=vh, hg=hg):
                                for kc in range(NF):
                                    e.matmul(ps_a[:, 0:256], cur[:, kc, ts(tt)],
                                             WVB[:, kc, :], start=(kc == 0), stop=False)
                                row = self.ROWS[:, ts(RS["bv1"], D)]
                                return e.matmul(ps_a[:, 0:256], self.ONES1[:, 0:128],
                                                row[:, 512 * hg + 256 * vh:512 * hg + 256 * (vh + 1)],
                                                start=False, stop=True)
                            mm = p.step("T", vmm, deps=[tr_tok, wv_tok, self.tok_const,
                                                        rows_tok,
                                                        self.free.get("ps_a")])
                            ev = p.step("V", lambda e, tb=tb, vh=vh: e.tensor_copy(
                                VG[:, tb, 256 * vh:256 * (vh + 1)], ps_a[:, 0:256]),
                                deps=[mm])
                            self.free["ps_a"] = ev
                            v_done.append(ev)
                    self.free["wvb"] = v_done[-1]
                for qt in range(NT):
                    nk = (qt + 1) * 128
                    cvs = []

                    def emit_pv(pend):
                        sc, a, lh, lft, pqt, pnk = pend
                        nck = (pqt + 1 + 3) // 4
                        pes = {}
                        tks = {}

                        def mk_pvm(ckg, n4, h):
                            def pvm(e):
                                last = None
                                for i in range(n4):
                                    kt = ckg * 4 + i
                                    last = e.matmul(
                                        ps_pv[64 * a:64 * a + 64, 0:128],
                                        VG[:, kt, 64 * lh:64 * lh + 64],
                                        PTB[h][:, ts(i)],
                                        start=(kt == 0), stop=(kt == pqt),
                                        tile_position=(0, 64 * a), skip_group_check=True)
                                return last
                            return pvm

                        def emit_pvm(ckg):
                            n4 = min(4, pqt + 1 - ckg * 4)
                            h = ckg % 2
                            tok = p.step("T", mk_pvm(ckg, n4, h),
                                         deps=[pes[ckg], v_done[-1:]] +
                                         ([self.free.get(("pspv", a))]
                                          if ckg == 0 else []))
                            self.free[("ptbb", h)] = tok
                            return tok

                        pv_last = None
                        for ckg in range(nck):
                            n4 = min(4, pqt + 1 - ckg * 4)
                            h = ckg % 2
                            if ckg >= 2:
                                pv_last = emit_pvm(ckg - 2)

                            def ptt(e, ckg=ckg, n4=n4, h=h):
                                last = None
                                for i in range(n4):
                                    kt = ckg * 4 + i
                                    last = e.transpose(PSB[h][:, ts(i)],
                                                       PB[:, ts(kt)], self.IDTB[:])
                                return last
                            tks[ckg] = p.step("T", ptt,
                                              deps=[sc, self.free.get(("pstb", h))])
                            pes[ckg] = p.step("S", lambda e, ckg=ckg, n4=n4, h=h:
                                e.activation(PTB[h][:, 0:n4 * 128],
                                             PSB[h][:, 0:n4 * 128], AF.Identity),
                                deps=[tks[ckg], self.free.get(("ptbb", h))])
                            self.free[("pstb", h)] = pes[ckg]
                        for ckg in range(max(0, nck - 2), nck):
                            pv_last = emit_pvm(ckg)
                        self.free["pb_r"] = tks[nck - 1]
                        cv = p.step("S", lambda e, a=a, lft=lft: e.activation(
                            CTC[64 * a:64 * a + 64, lft, :],
                            ps_pv[64 * a:64 * a + 64, 0:128], AF.Identity),
                            deps=[pv_last, self.free.get(("ctc", lft, a))])
                        self.free[("pspv", a)] = cv
                        cvs.append(cv)

                    for lh in range(8):
                        hgl = 8 * hg + lh
                        a = lh % 2
                        lft = lh // 2
                        b4 = hgl % 4

                        def smm(e, qt=qt, lh=lh, a=a, lft=lft, b4=b4, hgl=hgl, nk=nk):
                            last = None
                            for ck in range((nk + 511) // 512):
                                w = min(512, nk - ck * 512)
                                last = e.matmul(
                                    ps_s[:, ck * 512:ck * 512 + w],
                                    QTG[64 * a:64 * a + 64, lft, ts(qt)],
                                    KTG[64 * a:64 * a + 64, lft, ck * 512:ck * 512 + w],
                                    start=True, stop=False,
                                    tile_position=(64 * a, 0), skip_group_check=True)
                                last = e.matmul(
                                    ps_s[:, ck * 512:ck * 512 + w],
                                    self.SCONST[32 * b4:32 * b4 + 4, hgl // 4, 0:128],
                                    J4[32 * b4:32 * b4 + 4, ck * 512:ck * 512 + w],
                                    start=False, stop=True,
                                    tile_position=(32 * b4, 0), skip_group_check=True)
                            return last
                        mm = p.step("T", smm, deps=kq_done[-1:] + [self.free.get("pss"),
                                                                   self.tok_const, cm_toks])
                        if self._c_pend is not None:
                            emit_pv(self._c_pend)
                            self._c_pend = None
                        t = None
                        if qt:
                            t = p.step("V", lambda e, qt=qt: e.tensor_copy(
                                SB32[:, 0:qt * 128], ps_s[:, 0:qt * 128]), deps=[mm])
                        evc = p.step("V", lambda e, qt=qt: e.tensor_tensor(
                            SB32[:, ts(qt)], ps_s[:, ts(qt)], MMASK[:], ALU.add),
                            deps=[mm] if t is None else [t])
                        self.free["pss"] = evc
                        nm = p.step("V", lambda e, nk=nk: e.tensor_reduce(
                            self.SMALL[:, 0:1], SB32[:, 0:nk], AX.X, ALU.max, negate=True),
                            deps=[evc])
                        ex = p.step("S", lambda e, nk=nk: e.activation(
                            PB[:, 0:nk], SB32[:, 0:nk], AF.Exp, bias=self.SMALL[:, 0:1],
                            accum_out=self.SMALL[:, 2:3]),
                            deps=[nm, self.free.get("pb_r")])
                        rz = p.step("V", lambda e: e.reciprocal(
                            self.SMALL[:, 3:4], self.SMALL[:, 2:3]), deps=[ex])
                        sc = p.step("V", lambda e, nk=nk: e.tensor_scalar(
                            PB[:, 0:nk], PB[:, 0:nk], self.SMALL[:, 3:4], None, ALU.mult),
                            deps=[rz])
                        self._c_pend = (sc, a, lh, lft, qt, nk)
                    if self._c_pend is not None:
                        emit_pv(self._c_pend)
                        self._c_pend = None
                    if qt == 0:
                        self._wp_tok = [p.dma(
                            "G", WPB[:],
                            dt["Wproj"][:, 4 * hg:4 * hg + 4, :],
                            deps=[self.free.get("wpb")])]

                    for hf in range(2):
                        def pjm(e, hg=hg, hf=hf):
                            last = None
                            for lfc in range(4):
                                last = e.matmul(ps_pj[:], CTC[:, lfc, :],
                                                WPB[:, lfc, ts(hf, 512)],
                                                start=(lfc == 0), stop=(lfc == 3 and hg == 1))
                            if hg == 0:
                                row = self.ROWS[:, ts(RS["bproj"], D)]
                                last = e.matmul(ps_pj[:], self.ONES1[:, 0:128],
                                                row[:, ts(hf, 512)],
                                                start=False, stop=True)
                            return last
                        mm = p.step("T", pjm, deps=(cvs if hf == 0 else []) +
                                    list(self._wp_tok) + [rows_tok,
                                                          self.free.get("ps_a")])
                        ev = p.step("V", lambda e, qt=qt, hf=hf: e.scalar_tensor_tensor(
                            self.XR[:, qt, ts(hf, 512)], ps_pj[:], 1.0,
                            self.XR[:, qt, ts(hf, 512)],
                            ALU.bypass, ALU.add), deps=[mm, prj_toks[qt]])
                        self.free["ps_a"] = ev
                        prj_toks[qt] = ev
                    for lf in range(4):
                        for aa in range(2):
                            self.free[("ctc", lf, aa)] = ev
                self.free["wpb"] = prj_toks[-1]
            x3_toks = self.layernorm(self.XR, prj_toks, 1e-5)
            return x3_toks

    def transpose_into_b(self, src, dst, ps_tb, deps, tc):
        p = self.p
        last = None
        for fc in range(NF):
            tk = p.step("T", lambda e, fc=fc, tc=tc: [
                e.transpose(ps_tb[:, ts(tt)], src[:, tc * 4 + tt, ts(fc)], self.IDTB[:])
                for tt in range(4)][-1],
                deps=list(deps) + [self.free.get("pstb"),
                                   self.free.get(("pstb", 0)),
                                   self.free.get(("pstb", 1))])
            last = p.step("V", lambda e, fc=fc: e.tensor_copy(
                dst[:, fc, :], ps_tb[:, 0:512]), deps=[tk])
            self.free["pstb"] = last
        return last


    def _quant5(self, src_ap, scale_ap, clip_hi, clip_lo, out_ap, dep, n):
        p, TMP, SCR = self.p, self.TMP, self.SCR
        i32 = TMP[:, 0, 0:n].bitcast(I32)
        t = p.step("V", lambda e: e.tensor_scalar(
            i32, src_ap, scale_ap, 256.5, ALU.mult, ALU.add), deps=[dep])
        t = p.step("V", lambda e: e.tensor_scalar(
            TMP[:, 1, 0:n], i32, 256.5, None, ALU.subtract), deps=[t])
        t = p.step("V", lambda e: e.scalar_tensor_tensor(
            SCR[:, 0:n], src_ap, scale_ap, TMP[:, 1, 0:n], ALU.mult, ALU.is_lt),
            deps=[t])
        t = p.step("V", lambda e: e.scalar_tensor_tensor(
            TMP[:, 0, 0:n], TMP[:, 1, 0:n], 0.5, SCR[:, 0:n],
            ALU.add, ALU.subtract), deps=[t])
        t = p.step("V", lambda e: e.tensor_scalar(
            out_ap, TMP[:, 0, 0:n], clip_hi, clip_lo, ALU.min, ALU.max), deps=[t])
        return t

    def _emit_w0(self):
        p, dt, nc = self.p, self.dt, self.nc
        with contextlib.ExitStack() as st:
            ps = st.enter_context(nc.psum_tensor("W0_ps", [128, 512], F32))
            ps_tb = st.enter_context(nc.psum_tensor("W0_pstb", [128, 512], BF16))
            WST = [self._sbuf("W0_WST0", (128, 1024), F32, st),
                   self._sbuf("W0_WST1", (128, 1024), F32, st)]
            QB = self._sbuf("W0_QB", (128, 1024), BF16, st)
            QTT = [self._sbuf("W0_QTT0", (128, 8, 128), BF16, st),
                   self._sbuf("W0_QTT1", (128, 8, 128), BF16, st)]
            ASUM = self._sbuf("W0_ASUM", (128, 2), F32, st)
            t0 = p.step("V", lambda e: e.memset(ASUM[:], 0.0), deps=[self.tok_const])
            ci = 0
            for wi, wname, nrc, ncol in ((0, "f_w1", 32, 1024), (1, "f_w2", 8, 4096)):
                for rc in range(nrc):
                    for qc in range(ncol // 1024):
                        w = WST[ci % 2]
                        ld = p.dma("Y", w[:], dt[wname][128 * rc:128 * (rc + 1),
                                                        1024 * qc:1024 * (qc + 1)],
                                   deps=[self.free.get(("w0wst", ci % 2))])
                        t0 = p.step("V", lambda e, w=w: e.tensor_reduce(
                            self.SCR[:, 0:1], w[:], AX.X, ALU.add,
                            apply_absolute_value=True), deps=[ld, t0])
                        t0 = p.step("V", lambda e, wi=wi: e.tensor_tensor(
                            ASUM[:, wi:wi + 1], ASUM[:, wi:wi + 1], self.SCR[:, 0:1],
                            ALU.add), deps=[t0])
                        self.free[("w0wst", ci % 2)] = t0
                        ci += 1
            mt = p.step("T", lambda e: e.matmul(ps[0:2, 0:1], ASUM[:], self.ONE128[:],
                                                start=True, stop=True), deps=[t0])
            t = p.step("V", lambda e: e.tensor_scalar(
                self.SCR[0:2, 0:1], ps[0:2, 0:1], 1.0 / (4096 * 1024), 1e-5,
                ALU.mult, ALU.max), deps=[mt])
            t = p.step("V", lambda e: e.reciprocal(self.SCR[0:2, 1:2], self.SCR[0:2, 0:1]),
                       deps=[t])
            tk1 = p.step("T", lambda e: e.transpose(
                ps[0:1, 0:2], self.SCR[0:2, 0:1], self.IDT[0:2, 0:2]), deps=[t])
            ev1 = p.step("V", lambda e: e.tensor_copy(self.SCR[0:1, 4:6], ps[0:1, 0:2]),
                         deps=[tk1])
            tk2 = p.step("T", lambda e: e.transpose(
                ps[0:1, 2:4], self.SCR[0:2, 1:2], self.IDT[0:2, 0:2]), deps=[ev1])
            ev2 = p.step("V", lambda e: e.tensor_copy(self.SCR[0:1, 6:8], ps[0:1, 2:4]),
                         deps=[tk2])
            mt2 = p.step("T", lambda e: e.matmul(
                ps[:, 0:4], self.ONES1F[:], self.SCR[0:1, 4:8], start=True, stop=True),
                deps=[ev2])
            tb_ = p.step("V", lambda e: e.tensor_copy(self.SWB[:], ps[:, 0:4]), deps=[mt2])
            self.tok_const = self.tok_const + (tb_,)
            ci = 0
            for wi, wname, nrc, ncol in ((0, "f_w1", 32, 1024), (1, "f_w2", 8, 4096)):
                for rc in range(nrc):
                    for qc in range(ncol // 1024):
                        w = WST[ci % 2]
                        qtt = QTT[ci % 2]
                        ld = p.dma("Y", w[:], dt[wname][128 * rc:128 * (rc + 1),
                                                        1024 * qc:1024 * (qc + 1)],
                                   deps=[self.free.get(("w0wst", ci % 2))])
                        qt = self._quant5(w[:], self.SWB[:, 2 + wi:3 + wi], 1.0, -1.0,
                                          QB[:], (ld, tb_), 1024)
                        self.free[("w0wst", ci % 2)] = qt
                        evs = []
                        for g in range(2):
                            tk = p.step("T", lambda e, g=g: [
                                e.transpose(ps_tb[:, ts(i)], QB[:, ts(g * 4 + i)],
                                            self.IDTB[:]) for i in range(4)][-1],
                                deps=[qt, self.free.get("w0pstb"),
                                      self.free.get(("w0qtt", ci % 2))])
                            ev = p.step("V", lambda e, g=g, qtt=qtt: e.tensor_copy(
                                qtt[:, g * 4:g * 4 + 4, :],
                                ps_tb[:].rearrange("p (c w) -> p c w", c=4)), deps=[tk])
                            self.free["w0pstb"] = ev
                            evs.append(ev)
                        if wi == 0:
                            out_ap = dt["q1t_scr"][:, :, 128 * rc:128 * (rc + 1)]
                        else:
                            out_ap = dt["q2t_scr"][:, 8 * qc:8 * qc + 8,
                                                   128 * rc:128 * (rc + 1)]
                        st_tok = p.dma("G", out_ap, qtt[:], deps=evs)
                        self.free[("w0qtt", ci % 2)] = st_tok
                        ci += 1
        p.barrier()

    def emit_phaseD(self, toks):
        self._phase_reset()
        """XR holds x3 -> x4 (BitNet FFN + module LN)."""
        p, dt, nc = self.p, self.dt, self.nc
        with contextlib.ExitStack() as st:
            def psum(name, shape, dty=F32):
                return st.enter_context(nc.psum_tensor(self._uniq(name), list(shape), dty))
            sbuf = lambda n, sh, dty: self._sbuf(n, sh, dty, st)
            ps_a = psum("D_psa", (128, 1024))
            ps_h = psum("D_psh", (128, 512))
            ps_h2 = psum("D_psh2", (128, 512))
            ps_hs = [ps_h, ps_h2]
            ps_tb = psum("D_pstb", (128, 512), BF16)

            x_dep = list(toks) if isinstance(toks, (list, tuple)) else [toks]
            DST = sbuf("D_DST", (128, 80), F32)
            MAXACC = sbuf("D_MAXACC", (128, 2048), F32)
            stA = contextlib.ExitStack()
            sbufA = lambda n, sh, dty: self._sbuf(n, sh, dty, stA)
            DEQ1BC = sbufA("D_DEQ1BC", (128, 2048), F32)
            HROWS = [sbufA("D_HROW0", (128, 2048), F32),
                     sbufA("D_HROW1", (128, 2048), F32)]
            HR2S = [sbufA("D_HR20", (128, 2048), F32),
                    sbufA("D_HR21", (128, 2048), F32)]
            XQB = sbufA("D_XQB", (128, 1024), BF16)
            xqt = self.X0B[:].rearrange("p t d -> p (t d)").rearrange(
                "p (c w) -> p c w", c=8)

            t = x_dep
            for tb in range(NT):
                tt_ = p.step("V", lambda e, tb=tb: e.tensor_reduce(
                    DST[:, tb:tb + 1], self.XR[:, tb, :], AX.X, ALU.max,
                    apply_absolute_value=True), deps=t)
                t = [tt_]
            t = p.step("V", lambda e: e.tensor_scalar(
                DST[:, 16:32], DST[:, 0:16], 1e-5, None, ALU.max), deps=t)
            t = p.step("V", lambda e: e.reciprocal(DST[:, 32:48], DST[:, 16:32]), deps=[t])
            t = p.step("V", lambda e: e.tensor_scalar(
                DST[:, 32:48], DST[:, 32:48], 127.0, None, ALU.mult), deps=[t])
            t = p.step("V", lambda e: e.reciprocal(DST[:, 48:64], DST[:, 32:48]), deps=[t])
            tdq = p.step("V", lambda e: e.tensor_scalar(
                DST[:, 48:64], DST[:, 48:64], self.SWB[:, 0:1], None, ALU.mult),
                deps=[t, self.tok_const])
            d1 = p.dma("G", dt["deq_scr"].transpose([1, 0]), DST[:, 48:64], deps=[tdq],
                       slow=True)
            d2 = p.dma("G", self.TMP[0:1, 0, :],
                       dt["deq_scr"].rearrange("t p -> (t p)").unsqueeze(0)[:, 0:1024],
                       deps=[d1])
            d3 = p.dma("G", self.TMP[0:1, 1, :],
                       dt["deq_scr"].rearrange("t p -> (t p)").unsqueeze(0)[:, 1024:2048],
                       deps=[d2])
            bc_toks = []
            for i in range(4):
                src = self.TMP[0:1, i // 2, ts(i % 2, 512)]
                mt = p.step("T", lambda e, i=i, src=src: e.matmul(
                    ps_h[:], self.ONES1F[:], src, start=True, stop=True),
                    deps=[d3, self.free.get("ps_h")])
                ev = p.step("V", lambda e, i=i: e.tensor_copy(
                    DEQ1BC[:, ts(i, 512)], ps_h[:]), deps=[mt])
                self.free["ps_h"] = ev
                bc_toks.append(ev)
            xq_last = None
            for tb in range(NT):
                qt = self._quant5(self.XR[:, tb, :], DST[:, 32 + tb:33 + tb],
                                  127.0, -128.0, XQB[:],
                                  (tdq, tuple(bc_toks), self.free.get("xqb") or bc_toks[-1]),
                                  1024)
                self.free["xqb"] = qt
                for g in range(2):
                    tk = p.step("T", lambda e, g=g: [
                        e.transpose(ps_tb[:, ts(i)], XQB[:, ts(g * 4 + i)], self.IDTB[:])
                        for i in range(4)][-1], deps=[qt, self.free.get("pstb")])
                    ev = p.step("V", lambda e, g=g, tb=tb: e.tensor_copy(
                        xqt[:, g * 4:g * 4 + 4, ts(tb)],
                        ps_tb[:].rearrange("p (c w) -> p c w", c=4)), deps=[tk])
                    self.free["pstb"] = ev
                    self.free["xqb"] = ev
                    xq_last = ev
            mx0 = p.step("V", lambda e: e.memset(MAXACC[:], 0.0), deps=x_dep)
            Q1T = [sbufA("D_Q1T0", (128, 8, 1024), BF16),
                   sbufA("D_Q1T1", (128, 8, 1024), BF16)]
            for qf in range(4):
                q1tok = p.dma("G", Q1T[qf % 2][:],
                              dt["q1t_scr"][:, :, 1024 * qf:1024 * (qf + 1)],
                              deps=[self.free.get(("q1t", qf % 2))])
                last_mm = None
                for lft in range(8):
                    ht = qf * 8 + lft
                    bi = ht % 2
                    HROW, HR2 = HROWS[bi], HR2S[bi]
                    hmm_toks = []
                    for tc in range(4):
                        ph = ps_hs[tc % 2]

                        def hmm(e, lft=lft, tc=tc, qf=qf, ph=ph):
                            last = None
                            for kc in range(NF):
                                last = e.matmul(ph[:],
                                                Q1T[qf % 2][:, kc, 128 * lft:128 * (lft + 1)],
                                                xqt[:, kc, ts(tc, 512)],
                                                start=(kc == 0), stop=(kc == NF - 1))
                            return last
                        mm = p.step("T", hmm, deps=[q1tok, xq_last,
                                                    self.free.get(("ps_h", tc % 2))])
                        last_mm = mm
                        ev = p.step("V", lambda e, tc=tc, HROW=HROW, ph=ph: e.tensor_tensor(
                            HROW[:, ts(tc, 512)], ph[:], DEQ1BC[:, ts(tc, 512)],
                            ALU.mult), deps=[mm, bc_toks[tc],
                                             self.free.get(("hrow", bi))])
                        self.free[("ps_h", tc % 2)] = ev
                        hmm_toks.append(ev)
                    cb = ht
                    t1 = p.step("V", lambda e, cb=cb, HROW=HROW, HR2=HR2: e.tensor_scalar(
                        HR2[:], HROW[:], self.CWPP[:, 1, cb:cb + 1],
                        self.BPP[:, 48 + cb:49 + cb], ALU.mult, ALU.add),
                        deps=hmm_toks + [self.free.get(("hr2", bi))])
                    t1 = p.step("V", lambda e, cb=cb, HROW=HROW, HR2=HR2: e.scalar_tensor_tensor(
                        HR2[:, 1:2048], HROW[:, 0:2047], self.CWPP[:, 0, cb:cb + 1],
                        HR2[:, 1:2048], ALU.mult, ALU.add), deps=[t1])
                    t1 = p.step("V", lambda e, cb=cb, HROW=HROW, HR2=HR2: e.scalar_tensor_tensor(
                        HR2[:, 0:2047], HROW[:, 1:2048], self.CWPP[:, 2, cb:cb + 1],
                        HR2[:, 0:2047], ALU.mult, ALU.add), deps=[t1])
                    self.free[("hrow", bi)] = t1
                    sl = p.step("S", lambda e, HROW=HROW, HR2=HR2: e.activation(
                        HROW[:], HR2[:], AF.Silu), deps=[t1])
                    ab = p.step("V", lambda e, HROW=HROW, HR2=HR2: e.scalar_tensor_tensor(
                        HR2[:], HROW[:], -1.0, HROW[:], ALU.mult, ALU.max), deps=[sl, mx0])
                    mx0 = p.step("V", lambda e, HR2=HR2: e.tensor_tensor(
                        MAXACC[:], HR2[:], MAXACC[:], ALU.max), deps=[ab])
                    self.free[("hr2", bi)] = mx0
                    sp = p.dma("G", dt["h2_scr"][ht], HROW[:], deps=[mx0])
                    self.free[("hrow", bi)] = sp
                self.free[("q1t", qf % 2)] = last_mm
            s2t = []
            for tg in range(4):
                tk = p.step("T", lambda e, tg=tg: [
                    e.transpose(ps_h[:, ts(i)], MAXACC[:, ts(tg * 4 + i)],
                                self.IDT[:]) for i in range(4)][-1],
                    deps=[mx0, self.free.get("ps_h")])
                rd = p.step("V", lambda e, tg=tg: e.tensor_reduce(
                    DST[:, 64 + tg * 4:64 + tg * 4 + 4],
                    ps_h[:].rearrange("p (c w) -> p c w", c=4), AX.X, ALU.max),
                    deps=[tk])
                self.free["ps_h"] = rd
                s2t.append(rd)
            t = p.step("V", lambda e: e.tensor_scalar(
                DST[:, 0:16], DST[:, 64:80], 1e-5, None, ALU.max), deps=s2t)
            t = p.step("V", lambda e: e.reciprocal(DST[:, 16:32], DST[:, 0:16]), deps=[t])
            ts2 = p.step("V", lambda e: e.tensor_scalar(
                DST[:, 16:32], DST[:, 16:32], 127.0, None, ALU.mult), deps=[t])
            t = p.step("V", lambda e: e.reciprocal(DST[:, 32:48], DST[:, 16:32]), deps=[ts2])
            tdq2 = p.step("V", lambda e: e.tensor_scalar(
                DST[:, 32:48], DST[:, 32:48], self.SWB[:, 1:2], None, ALU.mult),
                deps=[t])
            d1 = p.dma("G", dt["deq_scr"].transpose([1, 0]), DST[:, 16:32],
                       deps=[ts2, tdq2], slow=True)
            d2 = p.dma("G", self.TMP[0:1, 0, :],
                       dt["deq_scr"].rearrange("t p -> (t p)").unsqueeze(0)[:, 0:1024],
                       deps=[d1])
            d3 = p.dma("G", self.TMP[0:1, 1, :],
                       dt["deq_scr"].rearrange("t p -> (t p)").unsqueeze(0)[:, 1024:2048],
                       deps=[d2])
            S2BC = MAXACC
            s2bc_toks = []
            for i in range(4):
                src = self.TMP[0:1, i // 2, ts(i % 2, 512)]
                mt = p.step("T", lambda e, i=i, src=src: e.matmul(
                    ps_h[:], self.ONES1F[:], src, start=True, stop=True),
                    deps=[d3, self.free.get("ps_h")])
                ev = p.step("V", lambda e, i=i: e.tensor_copy(
                    S2BC[:, ts(i, 512)], ps_h[:]), deps=[mt, mx0])
                self.free["ps_h"] = ev
                s2bc_toks.append(ev)
            stA.close()
            p.barrier()
            Q2T = sbuf("D_Q2T", (128, 8, 1024), BF16)
            Q2C = sbuf("D_Q2C", (128, 8, 512), BF16)
            H2T = [sbuf("D_H2T0", (128, 512), F32), sbuf("D_H2T1", (128, 512), F32)]
            acc_toks = [None] * NT
            for qf in range(4):
                q2tok = p.dma("G", Q2T[:], dt["q2t_scr"][:, 8 * qf:8 * (qf + 1), :],
                              deps=[self.free.get("q2t")])
                for tc in range(4):
                    qc_toks = []
                    for lhc in range(8):
                        ht = qf * 8 + lhc
                        hl = p.dma("Y", H2T[lhc % 2][:], dt["h2_scr"][ht][:, ts(tc, 512)],
                                   deps=[self.free.get(("h2t", lhc % 2))])
                        qt = self._quant6(H2T[lhc % 2][:], S2BC[:, ts(tc, 512)], Q2C,
                                          lhc, (hl, tuple(s2bc_toks),
                                                self.free.get("q2c") or s2bc_toks[-1]), 512)
                        self.free[("h2t", lhc % 2)] = qt
                        qc_toks.append(qt)
                    for tt in range(4):
                        tb = tc * 4 + tt

                        def ymm(e, tt=tt, qf=qf):
                            last = None
                            for hf in range(2):
                                for lhc in range(8):
                                    last = e.matmul(
                                        ps_a[:, ts(hf, 512)], Q2C[:, lhc, ts(tt)],
                                        Q2T[:, lhc, ts(hf, 512)],
                                        start=(lhc == 0), stop=(lhc == 7))
                            return last
                        mm = p.step("T", ymm, deps=qc_toks + [q2tok,
                                                              self.free.get("ps_a")])
                        if tt == 3:
                            self.free["q2c"] = mm
                            if tc == 3:
                                self.free["q2t"] = mm
                        ev = p.step("V", lambda e, tb=tb: e.scalar_tensor_tensor(
                            self.XR[:, tb, :], ps_a[:], DST[:, 32 + tb:33 + tb],
                            self.XR[:, tb, :], ALU.mult, ALU.add),
                            deps=[mm, acc_toks[tb] or tdq2])
                        self.free["ps_a"] = ev
                        acc_toks[tb] = ev
            x4_toks = self.layernorm(self.XR, acc_toks, 1e-5)
            return x4_toks

    def _quant6(self, src_ap, scale_row_ap, q2c, lhc, dep, n):
        p, TMP = self.p, self.TMP
        t = p.step("V", lambda e: e.tensor_tensor(
            TMP[:, 0, 0:n], src_ap, scale_row_ap, ALU.mult), deps=[dep])
        i32 = TMP[:, 1, 0:n].bitcast(I32)
        t = p.step("V", lambda e: e.tensor_scalar(
            i32, TMP[:, 0, 0:n], 256.5, None, ALU.add), deps=[t])
        t = p.step("V", lambda e: e.tensor_scalar(
            TMP[:, 0, 512:512 + n], i32, 256.5, None, ALU.subtract), deps=[t])
        t = p.step("V", lambda e: e.tensor_tensor(
            TMP[:, 1, 0:n], TMP[:, 0, 0:n], TMP[:, 0, 512:512 + n], ALU.is_lt), deps=[t])
        t = p.step("V", lambda e: e.scalar_tensor_tensor(
            TMP[:, 0, 0:n], TMP[:, 0, 512:512 + n], 0.5, TMP[:, 1, 0:n],
            ALU.add, ALU.subtract), deps=[t])
        t = p.step("V", lambda e, lhc=lhc: e.tensor_scalar(
            q2c[:, lhc, :], TMP[:, 0, 0:n], 127.0, -128.0, ALU.min, ALU.max), deps=[t])
        return t

    def _emit(self):
        p, dt = self.p, self.dt
        self.load_consts()
        for b in range(self.batches):
            self._cur_batch = b
            x_tok = p.dma("Y", self.XR[:].rearrange("p t d -> p (t d)"), dt["x"][b],
                          deps=[self.free.get("xr_store")])
            toks = [x_tok] * NT
            for ph, dump in [("A", "x1"), ("B", "x2"), ("C", "x3"), ("D", None)]:
                if ph in self.phases:
                    toks = getattr(self, "emit_phase" + ph)(toks)
                    p.barrier()
                    toks = [("V", self.p.cnt["V"])] * NT
                if dump and dump in self.dumps:
                    dtok = p.dma("G", dt["dump_" + dump][b],
                                 self.XR[:].rearrange("p t d -> p (t d)"), deps=toks)
                    self.store_toks.append(dtok)
                    p._wait("G", dtok)
            stok = p.dma("G", dt["out"][b], self.XR[:].rearrange("p t d -> p (t d)"),
                         deps=toks)
            self.store_toks.append(stok)
            self.free["xr_store"] = stok
        p.waits("G", self.store_toks)

    def build(self):
        with self.stack:
            with self.nc.Block() as block:
                self.p.finalize(block)
        return self.nc



def prepare_shared_inputs(ins):
    g = {}
    C = np.ascontiguousarray
    def pcf(W):
        K, F = W.shape
        return C(W.reshape(K // 128, 128, F).transpose(1, 0, 2))
    g["Wq"] = pcf(ins["L0_attn_Wq"])
    g["Wk"] = pcf(ins["L0_attn_Wk"])
    g["Wv"] = pcf(ins["L0_attn_Wv"])
    g["Wo"] = pcf(ins["L0_attn_Wo"])
    W1 = ins["L0_attn_W1"]
    g["W1f"] = C(W1.reshape(8, 128, 32, 128).transpose(2, 1, 0, 3))
    g["W2"] = pcf(ins["L0_attn_W2"])
    for n, k in [("bq", "L0_attn_bq"), ("bk", "L0_attn_bk"), ("bv", "L0_attn_bv"),
                 ("bo", "L0_attn_bo"), ("b2", "L0_attn_b2")]:
        g[n + "_row"] = C(ins[k].reshape(1, D))
    g["b1_pp"] = C(ins["L0_attn_b1"].reshape(32, 128).T)
    for n, k in [("g1", "L0_attn_g1"), ("be1", "L0_attn_be1"),
                 ("g2", "L0_attn_g2"), ("be2", "L0_attn_be2")]:
        g[n + "_bc"] = C(np.broadcast_to(ins[k], (128, D)))
    g["choice_wT"] = pcf(ins["L0_ffn_choice_w"].T)
    w1 = ins["L0_ffn_w1"]
    g["moe_w1f"] = C(w1.reshape(8, 8, 128, 16, 128).transpose(0, 2, 3, 1, 4))
    g["moe_w2"] = C(ins["L0_ffn_w2"].reshape(8, 16, 128, D).transpose(0, 2, 1, 3))
    Wkqv = ins["L1_attn_W_kqv"]
    g["Wkqv_f"] = C(Wkqv[:, :2048].reshape(8, 128, 16, 128).transpose(2, 1, 0, 3))
    g["Wv1"] = pcf(Wkqv[:, 2048:])
    g["Wproj"] = pcf(ins["L1_attn_W_proj"])
    g["bv1_row"] = C(ins["L1_attn_b_kqv"][2048:].reshape(1, D))
    g["bproj_row"] = C(ins["L1_attn_b_proj"].reshape(1, D))
    g["slopes_row"] = C(ins["L1_attn_slopes"].reshape(1, 16))
    g["f_w1"] = C(ins["L1_ffn_w1"])
    g["f_w2"] = C(ins["L1_ffn_w2"])
    g["f_convw_pp"] = C(ins["L1_ffn_conv_w"][:, 0, :].reshape(32, 128, 3).transpose(1, 2, 0))
    g["f_convb_pp"] = C(ins["L1_ffn_conv_b"].reshape(32, 128).T)
    bk1 = ins["L1_attn_b_kqv"][:D]
    bq1 = ins["L1_attn_b_kqv"][D:2 * D]
    g["bk1_pp"] = C(bk1.reshape(8, 128).T)
    g["bq1_pp"] = C(bq1.reshape(8, 128).T)
    g.update(host_constants())
    return g


def shard_x(x, c):
    xc = x[BPC * c:BPC * (c + 1)]
    return np.ascontiguousarray(
        xc.reshape(BPC, NT, 128, D).transpose(0, 2, 1, 3).reshape(BPC, 128, NT * D))


def unshard_out(res_list):
    outs = []
    for r in res_list:
        o = r["out"].reshape(BPC, 128, NT, D).transpose(0, 2, 1, 3).reshape(BPC, S, D)
        outs.append(o)
    return np.concatenate(outs, 0)


def kernel(**inputs):
    core = Core(phases=("A", "B", "C", "D"))
    nc = core.build()
    shared = prepare_shared_inputs(inputs)
    in_maps = []
    for c in range(NCORES):
        m = dict(shared)
        m["x"] = shard_x(np.asarray(inputs["L0_attn_x"]), c)
        in_maps.append(m)
    res = run_bass_kernel_spmd(nc, in_maps, core_ids=list(range(NCORES)))
    return unshard_out(res.results).astype(np.float32)
